# version 15
# baseline (speedup 1.0000x reference)
import numpy as np
import concourse.bass as bass
import concourse.mybir as mybir
from concourse.bass_utils import run_bass_kernel_spmd
from contextlib import ExitStack

F32 = mybir.dt.float32
BF16 = mybir.dt.bfloat16
ALU = mybir.AluOpType
ACT = mybir.ActivationFunctionType
AX = mybir.AxisListType.X

D = 1024
H = 16
DH = 64
NPAIR = 8
SEQ = 4096
QTOK = 2048
NKT = SEQ // 128
NQT = QTOK // 128
QS = 1024
NQS = QTOK // QS
NTQ = QS // 128
IFF = 2816
GB = 16
MID2 = [0.0625, 0.5625, 1.5625, 3.0625, 6.25, 12.25, 25.0]
WQ_UNITS = (
    [("wg", i, 0, 1024) for i in range(22)]
    + [("wu", i, 0, 1024) for i in range(22)]
    + [("wd", i, s * 1024, 1024 if s < 2 else 768) for i in range(8) for s in range(3)]
)
N_WQ = len(WQ_UNITS)
DEBUG = [False]
WQ_UNIT_IDX = {(nm, ti, co): i for i, (nm, ti, co, _w) in enumerate(WQ_UNITS)}


class Sem:
    REGISTRY = None

    def __init__(self, stack, nc, name):
        self.h = stack.enter_context(nc.semaphore(name))
        self.n = 0
        if Sem.REGISTRY is not None:
            Sem.REGISTRY.append(self.h)

    def inc(self, instr, k=1):
        self.n += k
        instr.then_inc(self.h, k)
        return self.n


def build_core():
    nc = bass.Bass("TRN2", target_bir_lowering=False,
                   detect_race_conditions=False)

    x = nc.dram_tensor("x", [SEQ, D], F32, kind="ExternalInput").ap()
    w_q = nc.dram_tensor("w_q", [D, D], F32, kind="ExternalInput").ap()
    w_kv = nc.dram_tensor("w_kv", [D, 2 * D], F32, kind="ExternalInput").ap()
    w_out = nc.dram_tensor("w_out", [D, D], F32, kind="ExternalInput").ap()
    b_out = nc.dram_tensor("b_out", [1, D], F32, kind="ExternalInput").ap()
    scal = nc.dram_tensor("scal", [1, H], F32, kind="ExternalInput").ap()
    wgi = nc.dram_tensor("wg", [IFF, D], F32, kind="ExternalInput").ap()
    wui = nc.dram_tensor("wu", [IFF, D], F32, kind="ExternalInput").ap()
    wdi = nc.dram_tensor("wd", [D, IFF], F32, kind="ExternalInput").ap()
    eye = nc.dram_tensor("eye", [128, 128], F32, kind="ExternalInput").ap()
    y = nc.dram_tensor("y", [QTOK, D], F32, kind="ExternalOutput").ap()

    ktr_d = nc.dram_tensor("ktr_d", [NPAIR, 128, SEQ], BF16, kind="Internal").ap()
    qtr_d = nc.dram_tensor("qtr_d", [NPAIR, 128, QTOK], BF16, kind="Internal").ap()
    r1_d = nc.dram_tensor("r1_d", [QTOK, D], F32, kind="Internal").ap()
    wgq_d = nc.dram_tensor("wgq_d", [IFF, D], BF16, kind="Internal").ap()
    wuq_d = nc.dram_tensor("wuq_d", [IFF, D], BF16, kind="Internal").ap()
    wdq_d = nc.dram_tensor("wdq_d", [D, IFF], BF16, kind="Internal").ap()
    wq_in_d = {"wg": wgi, "wu": wui, "wd": wdi}
    wq_out_d = {"wg": wgq_d, "wu": wuq_d, "wd": wdq_d}
    dbg = {}
    if DEBUG[0]:
        dbg["ktr"] = nc.dram_tensor("dbg_ktr", [NPAIR, 128, SEQ], BF16,
                                    kind="ExternalOutput").ap()
        dbg["qtr"] = nc.dram_tensor("dbg_qtr", [NPAIR, 128, QTOK], BF16,
                                    kind="ExternalOutput").ap()
        dbg["r1"] = nc.dram_tensor("dbg_r1", [QTOK, D], F32,
                                   kind="ExternalOutput").ap()
        dbg["wgq"] = nc.dram_tensor("dbg_wgq", [IFF, D], BF16,
                                    kind="ExternalOutput").ap()
        dbg["v"] = nc.dram_tensor("dbg_v", [128, NKT, H, 65], BF16,
                                  kind="ExternalOutput").ap()
        dbg["kst"] = nc.dram_tensor("dbg_kst", [128, 4, H], F32,
                                    kind="ExternalOutput").ap()
        dbg["khat"] = nc.dram_tensor("dbg_khat", [128, 2, D], BF16,
                                     kind="ExternalOutput").ap()
        dbg["ktr01"] = nc.dram_tensor("dbg_ktr01", [2, 128, 8, 128], BF16,
                                      kind="ExternalOutput").ap()
        dbg["st0"] = nc.dram_tensor("dbg_st0", [128, 8, 64], F32,
                                    kind="ExternalOutput").ap()
        dbg["p2"] = nc.dram_tensor("dbg_p2", [128, 6, 512], BF16,
                                   kind="ExternalOutput").ap()
        dbg["wgT"] = nc.dram_tensor("dbg_wgT", [128, 8, IFF], BF16,
                                    kind="ExternalOutput").ap()
        dbg["wdT"] = nc.dram_tensor("dbg_wdT", [128, 22, D], BF16,
                                    kind="ExternalOutput").ap()
        dbg["xqt"] = nc.dram_tensor("dbg_xqt", [128, 2, 8, 128], BF16,
                                    kind="ExternalOutput").ap()
        dbg["y1b"] = nc.dram_tensor("dbg_y1b", [128, 2, D], F32,
                                    kind="ExternalOutput").ap()
        dbg["hqt"] = nc.dram_tensor("dbg_hqt", [128, 1, 4, 128], BF16,
                                    kind="ExternalOutput").ap()
        dbg["rlnv"] = nc.dram_tensor("dbg_rlnv", [128, 2, D], F32,
                                     kind="ExternalOutput").ap()

    st = ExitStack()
    with st:
        e = st.enter_context
        Sem.REGISTRY = []

        def S(name):
            return Sem(st, nc, name)

        eye_sb = e(nc.sbuf_tensor("eye_sb", [128, 128], BF16))
        ebc = e(nc.sbuf_tensor("ebc", [128, H], F32))
        negm = e(nc.sbuf_tensor("negm", [128, H], F32))
        negmsh = e(nc.sbuf_tensor("negmsh", [128, 8], F32))
        bout_bc = e(nc.sbuf_tensor("bout_bc", [128, D], F32))
        qbA = e(nc.sbuf_tensor("qbA", [128, 1024], F32))
        qh1 = e(nc.sbuf_tensor("qh1", [128, 1024], BF16))
        qh2 = e(nc.sbuf_tensor("qh2", [128, 1024], BF16))
        qh3 = e(nc.sbuf_tensor("qh3", [128, 1024], BF16))
        qh4 = e(nc.sbuf_tensor("qh4", [128, 1024], BF16))
        qg1 = e(nc.sbuf_tensor("qg1", [128, 64], F32))
        qg2 = e(nc.sbuf_tensor("qg2", [128, 64], F32))
        wstg = e(nc.sbuf_tensor("wstg", [128, 2, 1024], F32))
        wqo = e(nc.sbuf_tensor("wqo", [128, 2, 1024], BF16))
        lnm = e(nc.sbuf_tensor("lnm", [128, 8], F32))
        bst = e(nc.sbuf_tensor("bst", [128, 2, 6], F32))
        bagg = e(nc.sbuf_tensor("bagg", [128, 2], F32))
        rtw = e(nc.sbuf_tensor("rtw", [128, 8], F32))
        rsw = e(nc.sbuf_tensor("rsw", [128, 8], F32))
        dbgsb = e(nc.sbuf_tensor("dbgsb", [128, 8, 64], F32))
        dbgp2 = e(nc.sbuf_tensor("dbgp2", [128, 6, 512], BF16))
        lnx = e(nc.sbuf_tensor("lnx", [128, 1024], F32))

        vself = S("vself")
        sself = S("sself")
        s_misc = S("s_misc")
        s_set = S("s_set")
        p1bar = S("p1bar")
        p2bar = S("p2bar")
        s_x = S("s_x")
        s_xp = [S("s_xp0"), S("s_xp1")]
        s_w = S("s_w")
        s_w2 = S("s_w2")
        s_xt = S("s_xt")
        s_xte = S("s_xte")
        s_kv = S("s_kv")
        s_qm = S("s_qm")
        s_sq = S("s_sq")
        s_vst = S("s_vst")
        s_ssq = S("s_ssq")
        s_nrm = S("s_nrm")
        s_ktw = S("s_ktw")
        s_qtw = S("s_qtw")
        s_ktwp = [S("s_ktwp0"), S("s_ktwp1")]
        s_qtwp = [S("s_qtwp0"), S("s_qtwp1")]
        s_wfi = S("s_wfi")
        s_wfq = S("s_wfq")
        s_wfw = S("s_wfw")
        s_wfip = [S("s_wfip0"), S("s_wfip1")]
        s_wfwp = [S("s_wfwp0"), S("s_wfwp1")]
        s_kin = S("s_kin")
        s_qin = S("s_qin")
        s_kinp = [S("s_kinp0"), S("s_kinp1")]
        s_qinp = [S("s_qinp0"), S("s_qinp1")]
        s_simA = S("s_simA")
        s_simB = S("s_simB")
        s_expA = S("s_expA")
        s_expB = S("s_expB")
        s_avA = S("s_avA")
        s_avB = S("s_avB")
        s_oev = S("s_oev")
        s_otr = S("s_otr")
        s_ofin = S("s_ofin")
        s_obt = S("s_obt")
        s_obe = S("s_obe")
        s_op = S("s_op")
        s_xr = S("s_xr")
        s_xrp = [S("s_xrp0"), S("s_xrp1")]
        s_r1 = S("s_r1")
        s_r1w = S("s_r1w")
        s_r1wp = [S("s_r1wp0"), S("s_r1wp1")]
        s_r1i = S("s_r1i")
        s_r1ip = [S("s_r1ip0"), S("s_r1ip1")]
        s_wqi = S("s_wqi")
        s_trp = S("s_trp")
        s_tre = S("s_tre")
        s_xq3 = S("s_xq3")
        s_gu = S("s_gu")
        s_sil = S("s_sil")
        s_h = S("s_h")
        s_hq = S("s_hq")
        s_dn = S("s_dn")
        s_y2e = S("s_y2e")
        s_ln2 = S("s_ln2")
        s_out = S("s_out")
        s_outp = [S("s_outp0"), S("s_outp1")]

        with nc.Block() as blk0:
            @blk0.sync
            def _(sy):
                for hh_ in Sem.REGISTRY:
                    sy.sem_clear(hh_)

        blk = e(nc.Block())

        def vedge(vec):
            c = vself.inc(vec.engine_nop())
            vec.wait_ge(vself.h, c)

        def vrecip(vec, out, in_):
            vedge(vec)
            vec.reciprocal(out, in_)
            vedge(vec)

        def quant_slab(vec, src_ap, dst_ap, width):
            G = width // GB
            a = qbA.ap()[:, 0:width]
            sq = qh1.ap()[:, 0:width]
            b = qh2.ap()[:, 0:width]
            c = qh3.ap()[:, 0:width]
            d = qh4.ap()[:, 0:width]
            g1 = qg1.ap()[:, 0:G]
            g2 = qg2.ap()[:, 0:G]
            srcg = src_ap.rearrange("p (g s) -> p g s", s=GB)
            vec.tensor_reduce(g1, srcg, AX, ALU.max, apply_absolute_value=True)
            vedge(vec)
            vec.tensor_scalar(g1, g1, 1.0 / 6.0, 1e-12, ALU.mult, ALU.max)
            vrecip(vec, g2, g1)
            vec.tensor_tensor(g2, g2, g2, ALU.mult)
            vedge(vec)
            g2b = g2.unsqueeze(2).broadcast_to([128, G, GB])
            g1b = g1.unsqueeze(2).broadcast_to([128, G, GB])
            ag = a.rearrange("p (g s) -> p g s", s=GB)
            sqg = sq.rearrange("p (g s) -> p g s", s=GB)
            vec.tensor_tensor(a, src_ap, src_ap, ALU.mult)
            vec.tensor_tensor(ag, ag, g2b, ALU.mult)
            vec.tensor_scalar(b, a, MID2[0], None, ALU.is_gt)
            for m2 in MID2[1:]:
                vec.scalar_tensor_tensor(b, a, m2, b, ALU.is_gt, ALU.add)
            vec.tensor_scalar(c, b, 4.0, 0.0, ALU.subtract, ALU.max)
            vec.tensor_scalar(d, b, 6.0, 0.0, ALU.subtract, ALU.max)
            vec.scalar_tensor_tensor(c, d, 2.0, c, ALU.mult, ALU.add)
            vec.tensor_tensor(c, c, b, ALU.add)
            vec.tensor_scalar(a, src_ap, 0.0, None, ALU.is_ge)
            vec.scalar_tensor_tensor(ag, ag, 0.5, g1b, ALU.subtract, ALU.mult)
            return vec.tensor_tensor(dst_ap, c, a, ALU.mult)

        wq_next = [0]

        def emit_wq_unit():
            u = wq_next[0]
            if u >= N_WQ:
                return
            wq_next[0] = u + 1
            name, ti, co, w = WQ_UNITS[u]
            par = u % 2

            @blk.sync
            def _(sy):
                if u >= 2:
                    sy.wait_ge(s_wfq.h, u - 1)
                ins = sy.dma_start(
                    out=wstg.ap()[:, par, 0:w],
                    in_=wq_in_d[name][ti * 128:(ti + 1) * 128, co:co + w])
                s_wfip[u % 2].inc(ins, 16)

            @blk.vector
            def _(vec):
                vec.wait_ge(s_wfip[u % 2].h, 16 * (u // 2 + 1))
                if u >= 2:
                    vec.wait_ge(s_wfwp[u % 2].h, 16 * (u // 2))
                ins = quant_slab(vec, wstg.ap()[:, par, 0:w],
                                 wqo.ap()[:, par, 0:w], w)
                s_wfq.inc(ins)

            @blk.sync
            def _(sy):
                sy.wait_ge(s_wfq.h, u + 1)
                ins = sy.dma_start(
                    out=wq_out_d[name][ti * 128:(ti + 1) * 128, co:co + w],
                    in_=wqo.ap()[:, par, 0:w])
                s_wfwp[u % 2].inc(ins, 16)

        def ln_stats(vec, src_ap):
            vec.bn_stats(bst.ap()[:, 0, :], src_ap[:, 0:512])
            vec.bn_stats(bst.ap()[:, 1, :], src_ap[:, 512:1024])
            vedge(vec)
            vec.bn_aggr(bagg.ap(), bst.ap().rearrange("p a b -> p (a b)"))
            vedge(vec)
            ins = vec.tensor_scalar(
                rtw.ap(), bagg.ap()[:, 1:2].broadcast_to([128, 8]),
                1.0, 1e-5, ALU.mult, ALU.add)
            return vself.inc(ins)

        def ln_sqrt(sc, wait_c):
            sc.wait_ge(vself.h, wait_c)
            sc.activation(rsw.ap(), rtw.ap(), ACT.Ln)
            return sself.inc(sc.activation(rsw.ap(), rsw.ap(), ACT.Exp,
                                           scale=-0.5))

        def ln_fin(vec, src_ap, dst_ap, wait_s):
            vec.wait_ge(sself.h, wait_s)
            vec.tensor_scalar(lnx.ap(), src_ap, bagg.ap()[:, 0:1],
                              None, ALU.subtract)
            return vec.tensor_scalar(dst_ap, lnx.ap(), rsw.ap()[:, 0:1],
                                     None, ALU.mult)

        @blk.gpsimd
        def _(gp):
            s_misc.inc(gp.dma_start(out=eye_sb.ap(), in_=eye), 16)
            s_misc.inc(gp.dma_start(
                out=ebc.ap(), in_=scal.partition_broadcast(128).squeeze(1)), 16)
            s_misc.inc(gp.dma_start(
                out=bout_bc.ap(), in_=b_out.partition_broadcast(128).squeeze(1)), 16)

        @blk.scalar
        def _(sc):
            sc.wait_ge(s_misc.h, s_misc.n)
            ins = sc.activation(ebc.ap(), ebc.ap(), ACT.Exp)
            sself.inc(ins)

        @blk.vector
        def _(vec):
            if DEBUG[0]:
                vec.memset(dbgsb.ap(), 0.0)
                vec.memset(dbgp2.ap(), 0.0)
            vec.wait_ge(sself.h, sself.n)
            vec.tensor_scalar(negm.ap(), ebc.ap(), 80.0, -1.0,
                              ALU.min, ALU.mult)
            vec.tensor_tensor(lnm.ap()[:, 0:8], ebc.ap()[:, 0:8],
                              ebc.ap()[:, 8:16], ALU.max)
            vedge(vec)
            vec.tensor_tensor(lnm.ap()[:, 0:4], lnm.ap()[:, 0:4],
                              lnm.ap()[:, 4:8], ALU.max)
            vedge(vec)
            vec.tensor_tensor(lnm.ap()[:, 0:2], lnm.ap()[:, 0:2],
                              lnm.ap()[:, 2:4], ALU.max)
            vedge(vec)
            vec.tensor_tensor(lnm.ap()[:, 0:1], lnm.ap()[:, 0:1],
                              lnm.ap()[:, 1:2], ALU.max)
            vedge(vec)
            ins = vec.tensor_scalar(negmsh.ap(), 
                                    lnm.ap()[:, 0:1].broadcast_to([128, 8]),
                                    80.0, -1.0, ALU.min, ALU.mult)
            s_set.inc(ins)

        SETUP = s_set.n

        with ExitStack() as p12s:
            e12 = p12s.enter_context
            v_sb = e12(nc.sbuf_tensor("v_sb", [128, NKT, H, 65], BF16))

            with ExitStack() as p1s:
                e1 = p1s.enter_context
                xb = e1(nc.sbuf_tensor("xb", [128, 2, D], BF16))
                xt_sb = e1(nc.sbuf_tensor("xt_sb", [128, 2, 8, 128], BF16))
                wkv_sb = e1(nc.sbuf_tensor("wkv_sb", [128, 8, 2 * D], BF16))
                wqp_sb = e1(nc.sbuf_tensor("wqp_sb", [128, 8, D], BF16))
                ksq = e1(nc.sbuf_tensor("ksq", [128, D], F32))
                qsq = e1(nc.sbuf_tensor("qsq", [128, D], F32))
                kst = e1(nc.sbuf_tensor("kst", [128, 4, H], F32))
                qst = e1(nc.sbuf_tensor("qst", [128, 4, H], F32))
                khat = e1(nc.sbuf_tensor("khat", [128, 2, D], BF16))
                qhat = e1(nc.sbuf_tensor("qhat", [128, 2, D], BF16))
                ktr = e1(nc.sbuf_tensor("ktr", [128, 2, 8, 128], BF16))
                qtr = e1(nc.sbuf_tensor("qtr", [128, 2, 8, 128], BF16))
                ptx = e1(nc.psum_tensor("ptx", [128, 1024], F32))
                kv_ps = e1(nc.psum_tensor("kv_ps", [128, 2048], F32))
                q_ps = e1(nc.psum_tensor("q_ps", [128, 1024], F32))

                @blk.gpsimd
                def _(gp):
                    for cch in range(8):
                        s_w.inc(gp.dma_start(
                            out=wkv_sb.ap()[:, cch, :],
                            in_=w_kv[cch * 128:(cch + 1) * 128, :]), 16)
                        s_w.inc(gp.dma_start(
                            out=wqp_sb.ap()[:, cch, :],
                            in_=w_q[cch * 128:(cch + 1) * 128, :]), 16)

                @blk.vector
                def _(vec):
                    ins = vec.memset(v_sb.ap()[:, :, :, 64], 1.0)
                    s_set.inc(ins)
                P1SET = s_set.n

                grp = [0]

                def pe_tgroup(t, srcs):
                    g = grp[0]
                    grp[0] = g + 1
                    half = g % 2
                    if g >= 2:
                        t.wait_ge(s_xte.h, g - 1)
                    ins = None
                    for j, ap_ in enumerate(srcs):
                        ins = t.matmul(
                            ptx.ap()[:, half * 512 + j * 128:
                                     half * 512 + j * 128 + 128],
                            ap_, eye_sb.ap(), start=True, stop=True)
                    s_xt.inc(ins)
                    return g

                def vec_egroup(vec, g, dst_ap):
                    vec.wait_ge(s_xt.h, g + 1)
                    half = g % 2
                    src = ptx.ap()[:, half * 512:(half + 1) * 512].rearrange(
                        "p (c f) -> p c f", f=128)
                    ins = vec.tensor_copy(dst_ap, src)
                    s_xte.inc(ins)

                def gbase(kt):
                    return 6 * min(kt, NQT) + 4 * max(0, kt - NQT)

                for kt in range(NKT):
                    par = kt % 2
                    isq = kt < NQT
                    CQ = min(kt, NQT)
                    sq_base = kt + CQ
                    nrm_base = 2 * kt + CQ
                    gb = gbase(kt)
                    assert grp[0] == gb, (kt, grp[0], gb)

                    @blk.gpsimd
                    def _(gp):
                        if kt >= 2:
                            gp.wait_ge(s_xt.h, gbase(kt - 2) + 2)
                        ins = gp.dma_start(out=xb.ap()[:, par, :],
                                           in_=x[kt * 128:(kt + 1) * 128, :])
                        s_xp[kt % 2].inc(ins, 16)

                    @blk.tensor
                    def _(t):
                        if kt == 0:
                            t.wait_ge(s_misc.h, s_misc.n)
                            t.wait_ge(s_w.h, s_w.n)
                            t.wait_ge(s_set.h, P1SET)
                        t.wait_ge(s_xp[kt % 2].h, 16 * (kt // 2 + 1))
                        for g in range(2):
                            srcs = [xb.ap()[:, par,
                                            (g * 4 + j) * 128:(g * 4 + j + 1) * 128]
                                    for j in range(4)]
                            pe_tgroup(t, srcs)
                        t.wait_ge(s_xte.h, gb + 2)
                        if kt >= 1:
                            pb = 2 * (kt - 1) + min(kt - 1, NQT)
                            t.wait_ge(s_nrm.h, pb + 2)
                            t.wait_ge(s_sq.h, (kt - 1) + min(kt - 1, NQT) + 1)
                        ins = None
                        for cch in range(8):
                            for sl in range(4):
                                ins = t.matmul(
                                    kv_ps.ap()[:, sl * 512:(sl + 1) * 512],
                                    xt_sb.ap()[:, par, cch, :],
                                    wkv_sb.ap()[:, cch, sl * 512:(sl + 1) * 512],
                                    start=(cch == 0), stop=(cch == 7))
                        s_kv.inc(ins)
                        if isq:
                            if kt >= 1:
                                pb = 2 * (kt - 1) + min(kt - 1, NQT)
                                t.wait_ge(s_nrm.h, pb + 3)
                                t.wait_ge(s_sq.h,
                                          (kt - 1) + min(kt - 1, NQT) + 2)
                            ins = None
                            for cch in range(8):
                                for sl in range(2):
                                    ins = t.matmul(
                                        q_ps.ap()[:, sl * 512:(sl + 1) * 512],
                                        xt_sb.ap()[:, par, cch, :],
                                        wqp_sb.ap()[:, cch,
                                                    sl * 512:(sl + 1) * 512],
                                        start=(cch == 0), stop=(cch == 7))
                            s_qm.inc(ins)

                    @blk.scalar
                    def _(sc):
                        sc.wait_ge(s_kv.h, kt + 1)
                        s_sq.inc(sc.activation(ksq.ap(), kv_ps.ap()[:, 0:D],
                                               ACT.Square))
                        if isq:
                            sc.wait_ge(s_qm.h, kt + 1)
                            s_sq.inc(sc.activation(qsq.ap(), q_ps.ap(),
                                                   ACT.Square))
                        sc.wait_ge(s_vst.h, sq_base + 1)
                        sc.activation(kst.ap()[:, 1, :],
                                      kst.ap()[:, 0, :], ACT.Ln)
                        s_ssq.inc(sc.activation(kst.ap()[:, 1, :],
                                                kst.ap()[:, 1, :], ACT.Exp,
                                                scale=0.5))
                        if isq:
                            sc.wait_ge(s_vst.h, sq_base + 2)
                            sc.activation(qst.ap()[:, 1, :],
                                          qst.ap()[:, 0, :], ACT.Ln)
                            s_ssq.inc(sc.activation(qst.ap()[:, 1, :],
                                                    qst.ap()[:, 1, :], ACT.Exp,
                                                    scale=0.5))

                    @blk.vector
                    def _(vec):
                        for g in range(2):
                            vec_egroup(vec, gb + g,
                                       xt_sb.ap()[:, par, g * 4:(g + 1) * 4, :])
                        vec.wait_ge(s_kv.h, kt + 1)
                        if DEBUG[0] >= 2 and kt == 0:
                            vec.tensor_copy(dbgsb.ap()[:, 0, :],
                                            kv_ps.ap()[:, 0:64])
                            vec.tensor_copy(dbgsb.ap()[:, 1, :],
                                            kv_ps.ap()[:, D:D + 64])
                        ins = vec.tensor_copy(
                            v_sb.ap()[:, kt, :, 0:64],
                            kv_ps.ap()[:, D:2 * D].rearrange(
                                "p (h d) -> p h d", d=DH))
                        s_nrm.inc(ins)
                        vec.wait_ge(s_sq.h, sq_base + 1)
                        if DEBUG[0] >= 2 and kt == 0:
                            vec.tensor_copy(dbgsb.ap()[:, 2, :],
                                            ksq.ap()[:, 0:64])
                        vec.tensor_reduce(
                            kst.ap()[:, 0, :],
                            ksq.ap().rearrange("p (h d) -> p h d", d=DH),
                            AX, ALU.add)
                        s_vst.inc(vec.engine_nop())
                        vec.wait_ge(s_ssq.h, sq_base + 1)
                        if DEBUG[0] >= 2 and kt == 0:
                            vec.tensor_copy(dbgsb.ap()[:, 3, 0:16],
                                            kst.ap()[:, 0, :])
                            vec.tensor_copy(dbgsb.ap()[:, 3, 16:32],
                                            kst.ap()[:, 1, :])
                        vec.tensor_scalar(kst.ap()[:, 2, :], kst.ap()[:, 1, :],
                                          1e-12, None, ALU.max)
                        vrecip(vec, kst.ap()[:, 3, :], kst.ap()[:, 2, :])
                        if DEBUG[0] >= 2 and kt == 0:
                            vec.tensor_copy(dbgsb.ap()[:, 3, 32:48],
                                            kst.ap()[:, 3, :])
                        if kt >= 2:
                            vec.wait_ge(s_xt.h, gbase(kt - 2) + 4)
                        rb = kst.ap()[:, 3, :].unsqueeze(2).broadcast_to(
                            [128, H, DH])
                        ins = vec.tensor_tensor(
                            khat.ap()[:, par, :].rearrange(
                                "p (h d) -> p h d", d=DH),
                            kv_ps.ap()[:, 0:D].rearrange(
                                "p (h d) -> p h d", d=DH),
                            rb, ALU.mult)
                        s_nrm.inc(ins)
                        if isq:
                            vec.wait_ge(s_sq.h, sq_base + 2)
                            vec.tensor_reduce(
                                qst.ap()[:, 0, :],
                                qsq.ap().rearrange("p (h d) -> p h d", d=DH),
                                AX, ALU.add)
                            s_vst.inc(vec.engine_nop())
                            vec.wait_ge(s_ssq.h, sq_base + 2)
                            vec.tensor_scalar(qst.ap()[:, 2, :],
                                              qst.ap()[:, 1, :],
                                              1e-12, None, ALU.max)
                            vrecip(vec, qst.ap()[:, 3, :], qst.ap()[:, 2, :])
                            vec.tensor_tensor(qst.ap()[:, 3, :],
                                              qst.ap()[:, 3, :],
                                              ebc.ap(), ALU.mult)
                            if kt >= 2:
                                vec.wait_ge(s_xt.h, gbase(kt - 2) + 6)
                            rqb = qst.ap()[:, 3, :].unsqueeze(2).broadcast_to(
                                [128, H, DH])
                            ins = vec.tensor_tensor(
                                qhat.ap()[:, par, :].rearrange(
                                    "p (h d) -> p h d", d=DH),
                                q_ps.ap().rearrange("p (h d) -> p h d", d=DH),
                                rqb, ALU.mult)
                            s_nrm.inc(ins)

                    @blk.tensor
                    def _(t):
                        t.wait_ge(s_nrm.h, nrm_base + 2)
                        for g in range(2):
                            srcs = [khat.ap()[:, par,
                                              (g * 4 + j) * 128:
                                              (g * 4 + j + 1) * 128]
                                    for j in range(4)]
                            pe_tgroup(t, srcs)
                        if isq:
                            t.wait_ge(s_nrm.h, nrm_base + 3)
                            for g in range(2):
                                srcs = [qhat.ap()[:, par,
                                                  (g * 4 + j) * 128:
                                                  (g * 4 + j + 1) * 128]
                                        for j in range(4)]
                                pe_tgroup(t, srcs)

                    @blk.vector
                    def _(vec):
                        if kt >= 2:
                            vec.wait_ge(s_ktwp[kt % 2].h, 32 * (kt // 2))
                        if DEBUG[0] >= 2 and kt == 0:
                            vec.tensor_copy(dbgsb.ap()[:, 4, :],
                                            khat.ap()[:, par, 0:64])
                        for g in range(2):
                            vec_egroup(vec, gb + 2 + g,
                                       ktr.ap()[:, par, g * 4:(g + 1) * 4, :])
                        if DEBUG[0] >= 2 and kt == 0:
                            vec.tensor_copy(dbgsb.ap()[:, 5, :],
                                            ktr.ap()[:, par, 0, 0:64])
                        if isq:
                            if kt >= 2:
                                vec.wait_ge(s_qtwp[kt % 2].h, 32 * (kt // 2))
                            for g in range(2):
                                vec_egroup(vec, gb + 4 + g,
                                           qtr.ap()[:, par,
                                                    g * 4:(g + 1) * 4, :])

                    @blk.sync
                    def _(sy):
                        sy.wait_ge(s_xte.h, gb + 4)
                        if DEBUG[0] >= 2 and kt < 2:
                            sy.dma_start(out=dbg["ktr01"][kt, :, :, :],
                                         in_=ktr.ap()[:, par, :, :]).then_inc(
                                s_xr.h, 16)
                            s_xr.n += 16
                        for hseg in range(2):
                            src = ktr.ap()[:, par, hseg * 4:(hseg + 1) * 4, :]
                            dst = ktr_d.transpose([1, 0, 2])[
                                :, hseg * 4:(hseg + 1) * 4,
                                kt * 128:(kt + 1) * 128]
                            s_ktwp[kt % 2].inc(sy.dma_start(out=dst, in_=src), 16)
                        if isq:
                            sy.wait_ge(s_xte.h, gb + 6)
                            for hseg in range(2):
                                src = qtr.ap()[:, par,
                                               hseg * 4:(hseg + 1) * 4, :]
                                dst = qtr_d.transpose([1, 0, 2])[
                                    :, hseg * 4:(hseg + 1) * 4,
                                    kt * 128:(kt + 1) * 128]
                                s_qtwp[kt % 2].inc(sy.dma_start(out=dst, in_=src), 16)

                    if kt >= 4:
                        emit_wq_unit()

                @blk.vector
                def _(vec):
                    p1bar.inc(vec.engine_nop())

                @blk.scalar
                def _(sc):
                    p1bar.inc(sc.nop())

                @blk.sync
                def _(sy):
                    sy.wait_ge(s_ktwp[0].h, s_ktwp[0].n)
                    sy.wait_ge(s_ktwp[1].h, s_ktwp[1].n)
                    sy.wait_ge(s_qtwp[0].h, s_qtwp[0].n)
                    sy.wait_ge(s_qtwp[1].h, s_qtwp[1].n)
                    if DEBUG[0]:
                        sy.wait_ge(s_nrm.h, s_nrm.n)
                        for nm2, t2 in (("v", v_sb), ("kst", kst),
                                        ("khat", khat)):
                            sy.dma_start(out=dbg[nm2], in_=t2.ap()).then_inc(
                                s_xr.h, 16)
                            s_xr.n += 16
                        sy.wait_ge(s_xr.h, s_xr.n)
                    p1bar.inc(sy.nop())

                @blk.gpsimd
                def _(gp):
                    gp.wait_ge(s_xp[0].h, s_xp[0].n)
                    gp.wait_ge(s_xp[1].h, s_xp[1].n)
                    gp.wait_ge(s_w.h, s_w.n)
                    gp.wait_ge(s_misc.h, s_misc.n)
                    p1bar.inc(gp.engine_nop())

            QS2 = 512
            NQS2 = QTOK // QS2
            NT2 = QS2 // 128
            with ExitStack() as p2s:
                e2 = p2s.enter_context
                kpair = e2(nc.sbuf_tensor("kpair", [128, 2, SEQ], BF16))
                qpair = e2(nc.sbuf_tensor("qpair", [128, 2, QS2], BF16))
                ptile = e2(nc.sbuf_tensor("ptile", [128, 3, 2, QS2], BF16))
                ofm = e2(nc.sbuf_tensor("ofm", [65, 2, QS2], BF16))
                orc = e2(nc.sbuf_tensor("orc", [128, 1], F32))
                otok = e2(nc.sbuf_tensor("otok", [128, NT2, H, 64], BF16))
                wout_sb = e2(nc.sbuf_tensor("wout_sb", [128, 8, D], BF16))
                xresid = e2(nc.sbuf_tensor("xresid", [128, 2, D], F32))
                r1b = e2(nc.sbuf_tensor("r1b", [128, 2, D], F32))
                sAB = e2(nc.psum_tensor("sAB", [128, 2, 2, 512], F32))
                oA = e2(nc.psum_tensor("oA", [128, 512], F32))
                oB = e2(nc.psum_tensor("oB", [128, 512], F32))
                tailp = e2(nc.psum_tensor("tailp", [128, 1024], F32))
                oX = {0: oA, 1: oB}

                @blk.gpsimd
                def _(gp):
                    gp.wait_ge(p1bar.h, 4)
                    for cch in range(8):
                        s_w2.inc(gp.dma_start(
                            out=wout_sb.ap()[:, cch, :],
                            in_=w_out[cch * 128:(cch + 1) * 128, :]), 16)
                    p2bar.inc(gp.engine_nop())

                first = {"t": True, "sc": True, "sy": True, "v": True}

                def emit_kq_load(PIj):
                    qsj, pairj = divmod(PIj, NPAIR)
                    parj = PIj % 2

                    @blk.sync
                    def _(sy):
                        if first["sy"]:
                            sy.wait_ge(p1bar.h, 4)
                            first["sy"] = False
                        if PIj >= 2:
                            sy.wait_ge(s_simA.h, 32 * (PIj - 1))
                        ins = sy.dma_start(out=kpair.ap()[:, parj, :],
                                           in_=ktr_d[pairj, :, :])
                        s_kinp[parj].inc(ins, 16)
                        ins = sy.dma_start(
                            out=qpair.ap()[:, parj, :],
                            in_=qtr_d[pairj, :, qsj * QS2:(qsj + 1) * QS2])
                        s_qinp[parj].inc(ins, 16)

                emit_kq_load(0)
                for qs in range(NQS2):
                    for pair in range(NPAIR):
                        PI = qs * NPAIR + pair
                        par = PI % 2

                        @blk.tensor
                        def _(t):
                            if first["t"]:
                                t.wait_ge(p1bar.h, 4)
                                first["t"] = False
                            t.wait_ge(s_kinp[PI % 2].h, 16 * (PI // 2 + 1))
                            t.wait_ge(s_qinp[PI % 2].h, 16 * (PI // 2 + 1))
                            if pair == 0 and qs > 0:
                                t.wait_ge(s_r1.h, NT2 * qs)
                                t.wait_ge(s_obe.h, 2 * NT2 * qs)

                            def emit_sims(kt):
                                if kt >= 2:
                                    t.wait_ge(s_expA.h, PI * 32 + kt - 1)
                                elif PI > 0:
                                    t.wait_ge(s_expA.h, PI * 32)
                                ins = None
                                for hh in range(2):
                                    ins = t.matmul(
                                        sAB.ap()[:, kt % 2, hh, :],
                                        kpair.ap()[hh * 64:(hh + 1) * 64, par,
                                                   kt * 128:(kt + 1) * 128],
                                        qpair.ap()[hh * 64:(hh + 1) * 64,
                                                   par, :],
                                        start=True, stop=True,
                                        tile_position=(hh * 64, 0))
                                s_simA.inc(ins)

                            def emit_avs(kt):
                                t.wait_ge(s_expA.h, PI * 32 + kt + 1)
                                for hh in range(2):
                                    head = pair * 2 + hh
                                    if kt == 0 and PI >= 1:
                                        t.wait_ge(
                                            s_ofin.h,
                                            NT2 * (2 * (PI - 1) + hh + 1))
                                    ins = t.matmul(
                                        oX[hh].ap()[0:65, :],
                                        v_sb.ap()[:, kt, head, :],
                                        ptile.ap()[:, kt % 3, hh, :],
                                        start=(kt == 0), stop=(kt == NKT - 1))
                                    if hh == 1:
                                        s_avA.inc(ins)

                            for kt in range(NKT):
                                emit_sims(kt)
                                if kt >= 2:
                                    emit_avs(kt - 2)
                            emit_avs(NKT - 2)
                            emit_avs(NKT - 1)

                        @blk.scalar
                        def _(sc):
                            if first["sc"]:
                                sc.wait_ge(p1bar.h, 4)
                                first["sc"] = False
                            for kt in range(NKT):
                                sc.wait_ge(s_simA.h, PI * 32 + kt + 1)
                                if kt >= 3:
                                    sc.wait_ge(s_avA.h, PI * 32 + kt - 2)
                                ins = sc.activation(
                                    ptile.ap()[:, kt % 3, :, :].rearrange(
                                        "p a b -> p (a b)"),
                                    sAB.ap()[:, kt % 2, :, :].rearrange(
                                        "p a b -> p (a b)"),
                                    ACT.Exp,
                                    bias=negmsh.ap()[:, 0:1])
                                s_expA.inc(ins)

                        for hh in range(2):
                            head = pair * 2 + hh
                            obase = (2 * PI + hh) * NT2

                            @blk.vector
                            def _(vec):
                                if first["v"]:
                                    vec.wait_ge(p1bar.h, 4)
                                    first["v"] = False
                                vec.wait_ge(s_avA.h, (PI + 1) * 32)
                                if PI >= 1:
                                    vec.wait_ge(s_otr.h,
                                                (2 * (PI - 1) + hh + 1) * NT2)
                                ins = vec.tensor_copy(ofm.ap()[0:65, hh, :],
                                                      oX[hh].ap()[0:65, :])
                                s_oev.inc(ins)

                            @blk.tensor
                            def _(t):
                                t.wait_ge(s_oev.h, 2 * PI + hh + 1)
                                if PI == 0 and hh == 0 and qs == 0:
                                    pass
                                elif hh == 0 and pair == 0:
                                    t.wait_ge(s_obe.h, 2 * NT2 * qs)
                                for tq in range(NT2):
                                    gidx = obase + tq
                                    if gidx >= 2:
                                        t.wait_ge(s_ofin.h, gidx - 1)
                                    ins = t.matmul(
                                        tailp.ap()[:, (gidx % 2) * 512:
                                                   (gidx % 2) * 512 + 65],
                                        ofm.ap()[0:65, hh,
                                                 tq * 128:(tq + 1) * 128],
                                        eye_sb.ap()[0:65, 0:65],
                                        start=True, stop=True)
                                    s_otr.inc(ins)

                            @blk.vector
                            def _(vec):
                                for tq in range(NT2):
                                    vec.wait_ge(s_otr.h, obase + tq + 1)
                                    if qs > 0:
                                        vec.wait_ge(s_op.h, NT2 * qs)
                                    base = ((obase + tq) % 2) * 512
                                    vrecip(vec,
                                           orc.ap(),
                                           tailp.ap()[:, base + 64:base + 65])
                                    vedge(vec)
                                    ins = vec.tensor_scalar(
                                        otok.ap()[:, tq, head, :],
                                        tailp.ap()[:, base:base + 64],
                                        orc.ap(), None, ALU.mult)
                                    s_ofin.inc(ins)

                        if PI + 1 < NQS2 * NPAIR:
                            emit_kq_load(PI + 1)
                        emit_wq_unit()
                        emit_wq_unit()

                    for tq in range(NT2):
                        R = qs * NT2 + tq
                        par = R % 2

                        @blk.sync
                        def _(sy):
                            if R >= 2:
                                sy.wait_ge(s_r1.h, R - 1)
                            ins = sy.dma_start(out=xresid.ap()[:, par, :],
                                               in_=x[R * 128:(R + 1) * 128, :])
                            s_xrp[R % 2].inc(ins, 16)

                        og0 = (qs * NT2 + tq) * 2

                        @blk.tensor
                        def _(t):
                            if tq == 0:
                                t.wait_ge(s_ofin.h, (qs + 1) * 16 * NT2)
                                t.wait_ge(s_expA.h, (qs * NPAIR + NPAIR) * 32)
                            for g in range(2):
                                og = og0 + g
                                if og >= 2:
                                    t.wait_ge(s_obe.h, og - 1)
                                src_flat = otok.ap()[:, tq, :, :].rearrange(
                                    "p h d -> p (h d)")
                                ins = None
                                for j in range(4):
                                    cc = g * 4 + j
                                    ins = t.matmul(
                                        tailp.ap()[:, (og % 2) * 512 + j * 128:
                                                   (og % 2) * 512 + (j + 1) * 128],
                                        src_flat[:, cc * 128:(cc + 1) * 128],
                                        eye_sb.ap(), start=True, stop=True)
                                s_obt.inc(ins)

                        @blk.vector
                        def _(vec):
                            for g in range(2):
                                og = og0 + g
                                vec.wait_ge(s_obt.h, og + 1)
                                dst = otok.ap()[:, tq, :, :].rearrange(
                                    "p h d -> p (h d)").rearrange(
                                    "p (c f) -> p c f", f=128)[
                                    :, g * 4:(g + 1) * 4, :]
                                src = tailp.ap()[:, (og % 2) * 512:
                                                 (og % 2 + 1) * 512].rearrange(
                                    "p (c f) -> p c f", f=128)
                                ins = vec.tensor_copy(dst, src)
                                s_obe.inc(ins)

                        @blk.tensor
                        def _(t):
                            t.wait_ge(s_obe.h, og0 + 2)
                            if tq == 0:
                                t.wait_ge(s_oev.h, 2 * (qs * NPAIR + NPAIR))
                                t.wait_ge(s_w2.h, s_w2.n)
                            else:
                                t.wait_ge(s_r1.h, R)
                            src_flat = otok.ap()[:, tq, :, :].rearrange(
                                "p h d -> p (h d)")
                            ins = None
                            for cch in range(8):
                                for ds in range(2):
                                    ins = t.matmul(
                                        oX[ds].ap()[:, :],
                                        src_flat[:, cch * 128:(cch + 1) * 128],
                                        wout_sb.ap()[:, cch,
                                                     ds * 512:(ds + 1) * 512],
                                        start=(cch == 0), stop=(cch == 7))
                            s_op.inc(ins)

                        @blk.vector
                        def _(vec):
                            vec.wait_ge(s_op.h, R + 1)
                            vec.wait_ge(s_xrp[R % 2].h, 16 * (R // 2 + 1))
                            if R >= 2:
                                vec.wait_ge(s_r1wp[R % 2].h, 16 * (R // 2))
                            for ds in range(2):
                                vec.tensor_tensor(
                                    r1b.ap()[:, par, ds * 512:(ds + 1) * 512],
                                    oX[ds].ap(),
                                    bout_bc.ap()[:, ds * 512:(ds + 1) * 512],
                                    ALU.add)
                            ins = vec.tensor_tensor(r1b.ap()[:, par, :],
                                                    r1b.ap()[:, par, :],
                                                    xresid.ap()[:, par, :],
                                                    ALU.add)
                            s_r1.inc(ins)

                        @blk.sync
                        def _(sy):
                            sy.wait_ge(s_r1.h, R + 1)
                            ins = sy.dma_start(
                                out=r1_d[R * 128:(R + 1) * 128, :],
                                in_=r1b.ap()[:, par, :])
                            s_r1wp[R % 2].inc(ins, 16)

                        emit_wq_unit()

                @blk.vector
                def _(vec):
                    p2bar.inc(vec.engine_nop())

                @blk.scalar
                def _(sc):
                    p2bar.inc(sc.nop())

                @blk.sync
                def _(sy):
                    sy.wait_ge(s_r1wp[0].h, s_r1wp[0].n)
                    sy.wait_ge(s_r1wp[1].h, s_r1wp[1].n)
                    p2bar.inc(sy.nop())

                while wq_next[0] < N_WQ:
                    emit_wq_unit()

        with ExitStack() as p3s:
            e3 = p3s.enter_context
            wgT = e3(nc.sbuf_tensor("wgT", [128, 8, IFF], BF16))
            wuT = e3(nc.sbuf_tensor("wuT", [128, 8, IFF], BF16))
            wdT = e3(nc.sbuf_tensor("wdT", [128, 22, D], BF16))
            wqin = e3(nc.sbuf_tensor("wqin", [128, 1, 1024], BF16))
            wdin = e3(nc.sbuf_tensor("wdin", [128, 1, 4, 128], BF16))
            rln = e3(nc.sbuf_tensor("rln", [128, 2, D], F32))
            y1b = e3(nc.sbuf_tensor("y1b", [128, 2, D], F32))
            xqs = e3(nc.sbuf_tensor("xqs", [128, 1, D], BF16))
            xqt = e3(nc.sbuf_tensor("xqt", [128, 2, 8, 128], BF16))
            sgb = e3(nc.sbuf_tensor("sgb", [128, 1, 512], F32))
            hbuf = e3(nc.sbuf_tensor("hbuf", [128, 512], F32))
            hqb = e3(nc.sbuf_tensor("hqb", [128, 1, 512], BF16))
            hqt = e3(nc.sbuf_tensor("hqt", [128, 1, 4, 128], BF16))
            g_ps = e3(nc.psum_tensor("g_ps", [128, 1024], F32))
            u_ps = e3(nc.psum_tensor("u_ps", [128, 1024], F32))
            d_ps = e3(nc.psum_tensor("d_ps", [128, 1024], F32))
            trp = e3(nc.psum_tensor("trp", [128, 1024], F32))

            grp3 = [0]
            tre_of_grp = {}

            def pe_tgroup3(t, srcs):
                g = grp3[0]
                grp3[0] = g + 1
                half = g % 2
                if g >= 2:
                    t.wait_ge(s_tre.h, g - 1)
                ins = None
                for j, ap_ in enumerate(srcs):
                    ins = t.matmul(
                        trp.ap()[:, half * 512 + j * 128:
                                 half * 512 + j * 128 + 128],
                        ap_, eye_sb.ap(), start=True, stop=True)
                s_trp.inc(ins)
                return g

            def vec_egroup3(vec, g, dst_ap, nsub=4):
                vec.wait_ge(s_trp.h, g + 1)
                half = g % 2
                src = trp.ap()[:, half * 512:half * 512 + nsub * 128].rearrange(
                    "p (c f) -> p c f", f=128)
                ins = vec.tensor_copy(dst_ap, src)
                s_tre.inc(ins)

            def sc_egroup3(sc, g, dst_ap, nsub=4):
                sc.wait_ge(s_trp.h, g + 1)
                half = g % 2
                src = trp.ap()[:, half * 512:half * 512 + nsub * 128].rearrange(
                    "p (c f) -> p c f", f=128)
                ins = sc.activation(dst_ap, src, ACT.Copy)
                s_tre.inc(ins)

            firstp3 = {"t": True, "sy": True, "v": True, "sc": True}

            load_i = [0]
            wq_ready = {}

            def p3_load_unit(name, ti, co, w, dst_slot):
                li = load_i[0]
                load_i[0] = li + 1
                u = WQ_UNIT_IDX[(name, ti, co)]

                @blk.sync
                def _(sy):
                    if firstp3["sy"]:
                        sy.wait_ge(p2bar.h, 4)
                        sy.wait_ge(s_wfwp[0].h, s_wfwp[0].n)
                        sy.wait_ge(s_wfwp[1].h, s_wfwp[1].n)
                        firstp3["sy"] = False
                    if li >= 1:
                        sy.wait_ge(s_trp.h, wq_ready.get(("ld", li - 1), 0))
                    ins = sy.dma_start(
                        out=dst_slot,
                        in_=wq_out_d[name][ti * 128:(ti + 1) * 128, co:co + w])
                    s_wqi.inc(ins, 16)
                return li

            wgwu = {"wg": (wgT, wgq_d), "wu": (wuT, wuq_d)}
            order = []
            for sl in range(6):
                lo = sl * 4
                hi = min(lo + 4, 22)
                for nm in ("wg", "wu"):
                    for ti in range(lo, hi):
                        order.append((nm, ti))
            for nm, ti in order:
                par = 0
                li = p3_load_unit(nm, ti, 0, 1024, wqin.ap()[:, 0, :])
                gidx = []

                @blk.tensor
                def _(t):
                    if firstp3["t"]:
                        t.wait_ge(p2bar.h, 4)
                        firstp3["t"] = False
                    t.wait_ge(s_wqi.h, 16 * (li + 1))
                    for g in range(2):
                        srcs = [wqin.ap()[:, par,
                                          (g * 4 + j) * 128:(g * 4 + j + 1) * 128]
                                for j in range(4)]
                        gidx.append(pe_tgroup3(t, srcs))

                @blk.vector
                def _(vec):
                    if firstp3["v"]:
                        vec.wait_ge(p2bar.h, 4)
                        firstp3["v"] = False
                    dstT = wgT if nm == "wg" else wuT
                    for g in range(2):
                        vec_egroup3(vec, gidx[g],
                                    dstT.ap()[:, g * 4:(g + 1) * 4,
                                              ti * 128:(ti + 1) * 128])
                wq_ready[("ld", li)] = s_trp.n
                wq_ready[(nm, ti)] = s_tre.n

            for ti in range(8):
                for gc in range(6):
                    nch = 4 if gc < 5 else 2
                    par = 0
                    li = load_i[0]
                    load_i[0] += 1
                    u = WQ_UNIT_IDX[("wd", ti, (gc * 4 * 128 // 1024) * 1024)] \
                        if False else None

                    @blk.sync
                    def _(sy):
                        if li >= 1:
                            sy.wait_ge(s_trp.h, wq_ready.get(("ld", li - 1), 0))
                        for j in range(nch):
                            cc = gc * 4 + j
                            ins = sy.dma_start(
                                out=wdin.ap()[:, par, j, :],
                                in_=wdq_d[ti * 128:(ti + 1) * 128,
                                          cc * 128:(cc + 1) * 128])
                            s_wqi.inc(ins, 16)

                    gidx = []

                    @blk.tensor
                    def _(t):
                        t.wait_ge(s_wqi.h, s_wqi.n)
                        srcs = [wdin.ap()[:, par, j, :] for j in range(nch)]
                        gidx.append(pe_tgroup3(t, srcs))

                    @blk.vector
                    def _(vec):
                        vec_egroup3(vec, gidx[0],
                                    wdT.ap()[:, gc * 4:gc * 4 + nch,
                                             ti * 128:(ti + 1) * 128],
                                    nsub=nch)
                    wq_ready[("ld", li)] = s_trp.n
            WD_DONE = s_tre.n

            xqt_done = {}
            hq_trp = {}
            xqt_trp = {}

            def emit_rd(j):
                @blk.sync
                def _(sy):
                    if j == 0:
                        sy.wait_ge(s_r1wp[0].h, s_r1wp[0].n)
                        sy.wait_ge(s_r1wp[1].h, s_r1wp[1].n)
                    if j >= 2:
                        sy.wait_ge(s_xq3.h, j - 1)
                        sy.wait_ge(s_outp[j % 2].h, 16 * (j // 2))
                    ins = sy.dma_start(out=rln.ap()[:, j % 2, :],
                                       in_=r1_d[j * 128:(j + 1) * 128, :])
                    s_r1ip[j % 2].inc(ins, 16)

            def emit_ln1_stats(j):
                @blk.vector
                def _(vec):
                    vec.wait_ge(s_r1ip[j % 2].h, 16 * (j // 2 + 1))
                    ln_stats(vec, rln.ap()[:, j % 2, :])

            def emit_ln1_sqrt(j):
                @blk.scalar
                def _(sc):
                    if firstp3["sc"]:
                        sc.wait_ge(p2bar.h, 4)
                        firstp3["sc"] = False
                    ln_sqrt(sc, vself.n)

            def emit_ln1_finqx(j):
                @blk.vector
                def _(vec):
                    ln_fin(vec, rln.ap()[:, j % 2, :], y1b.ap()[:, j % 2, :],
                           sself.n)
                    if j >= 1:
                        vec.wait_ge(s_trp.h, xqt_trp.get(j - 1, 0))
                    ins = quant_slab(vec, y1b.ap()[:, j % 2, :],
                                     xqs.ap()[:, 0, :], D)
                    s_xq3.inc(ins)

            def emit_xqT(j):
                gidx = []

                @blk.tensor
                def _(t):
                    t.wait_ge(s_xq3.h, j + 1)
                    for g in range(2):
                        srcs = [xqs.ap()[:, 0,
                                         (g * 4 + k) * 128:
                                         (g * 4 + k + 1) * 128]
                                for k in range(4)]
                        gidx.append(pe_tgroup3(t, srcs))
                xqt_trp[j] = s_trp.n
                return gidx

            def emit_xqT_evac(j, gidx):
                @blk.scalar
                def _(sc):
                    for g in range(2):
                        sc_egroup3(sc, gidx[g],
                                   xqt.ap()[:, j % 2, g * 4:(g + 1) * 4, :])
                xqt_done[j] = s_tre.n

            def emit_gu(t_i, sl):
                W = 512 if sl < 5 else 256
                SL = 6 * t_i + sl
                p2_ = sl % 2
                par = t_i % 2

                @blk.tensor
                def _(t):
                    if sl == 0:
                        t.wait_ge(s_tre.h, xqt_done[t_i])
                    if t_i == 0:
                        t.wait_ge(s_tre.h,
                                  max(wq_ready[("wg", min(sl * 4 + 3, 21))],
                                      wq_ready[("wu", min(sl * 4 + 3, 21))]))
                    if SL >= 2:
                        t.wait_ge(s_sil.h, SL - 1)
                    ins = None
                    for cch in range(8):
                        ins = t.matmul(
                            g_ps.ap()[:, p2_ * 512:p2_ * 512 + W],
                            xqt.ap()[:, par, cch, :],
                            wgT.ap()[:, cch, sl * 512:sl * 512 + W],
                            start=(cch == 0), stop=(cch == 7))
                    s_gu.inc(ins)
                    if SL >= 2:
                        t.wait_ge(s_h.h, SL - 1)
                    for cch in range(8):
                        ins = t.matmul(
                            u_ps.ap()[:, p2_ * 512:p2_ * 512 + W],
                            xqt.ap()[:, par, cch, :],
                            wuT.ap()[:, cch, sl * 512:sl * 512 + W],
                            start=(cch == 0), stop=(cch == 7))
                    s_gu.inc(ins)

            def emit_silu(t_i, sl):
                W = 512 if sl < 5 else 256
                SL = 6 * t_i + sl
                p2_ = sl % 2

                @blk.scalar
                def _(sc):
                    sc.wait_ge(s_gu.h, 2 * SL + 1)
                    if SL >= 1:
                        sc.wait_ge(s_h.h, SL)
                    ins = sc.activation(sgb.ap()[:, 0, 0:W],
                                        g_ps.ap()[:, p2_ * 512:p2_ * 512 + W],
                                        ACT.Silu)
                    s_sil.inc(ins)

            def emit_hq(t_i, sl):
                W = 512 if sl < 5 else 256
                SL = 6 * t_i + sl
                p2_ = sl % 2

                @blk.vector
                def _(vec):
                    vec.wait_ge(s_gu.h, 2 * SL + 2)
                    vec.wait_ge(s_sil.h, SL + 1)
                    ins = vec.tensor_tensor(
                        hbuf.ap()[:, 0:W], sgb.ap()[:, 0, 0:W],
                        u_ps.ap()[:, p2_ * 512:p2_ * 512 + W], ALU.mult)
                    s_h.inc(ins)
                    if SL >= 1:
                        vec.wait_ge(s_trp.h, hq_trp.get(SL - 1, 0))
                    ins = quant_slab(vec, hbuf.ap()[:, 0:W],
                                     hqb.ap()[:, 0, 0:W], W)
                    s_hq.inc(ins)

            def emit_hqT(t_i, sl):
                W = 512 if sl < 5 else 256
                SL = 6 * t_i + sl
                nch = W // 128
                gidx2 = []

                @blk.tensor
                def _(t):
                    t.wait_ge(s_hq.h, SL + 1)
                    srcs = [hqb.ap()[:, 0, k * 128:(k + 1) * 128]
                            for k in range(nch)]
                    gidx2.append(pe_tgroup3(t, srcs))
                hq_trp[SL] = s_trp.n
                return gidx2

            def emit_hq_evac(t_i, sl, gidx2):
                W = 512 if sl < 5 else 256
                nch = W // 128

                @blk.scalar
                def _(sc):
                    sc_egroup3(sc, gidx2[0], hqt.ap()[:, 0, 0:nch, :],
                               nsub=nch)
                return s_tre.n

            def emit_down(t_i, sl, hq_evac):
                W = 512 if sl < 5 else 256
                nch = W // 128

                @blk.tensor
                def _(t):
                    t.wait_ge(s_tre.h, hq_evac)
                    if t_i == 0 and sl == 0:
                        t.wait_ge(s_tre.h, WD_DONE)
                    if sl == 0 and t_i >= 1:
                        t.wait_ge(s_y2e.h, t_i)
                    ins = None
                    for c4 in range(nch):
                        ic = sl * 4 + c4
                        for ds in range(2):
                            ins = t.matmul(
                                d_ps.ap()[:, ds * 512:(ds + 1) * 512],
                                hqt.ap()[:, 0, c4, :],
                                wdT.ap()[:, ic, ds * 512:(ds + 1) * 512],
                                start=(ic == 0), stop=(ic == 21))
                    if sl == 5:
                        s_dn.inc(ins)

            def emit_tail(t_i):
                par = t_i % 2

                @blk.vector
                def _(vec):
                    vec.wait_ge(s_dn.h, t_i + 1)
                    ins = vec.tensor_tensor(rln.ap()[:, par, :], d_ps.ap(),
                                            y1b.ap()[:, par, :], ALU.add)
                    s_y2e.inc(ins)
                    ln_stats(vec, rln.ap()[:, par, :])

                @blk.scalar
                def _(sc):
                    ln_sqrt(sc, vself.n)

                @blk.vector
                def _(vec):
                    ins = ln_fin(vec, rln.ap()[:, par, :],
                                 rln.ap()[:, par, :], sself.n)
                    s_ln2.inc(ins)

                @blk.sync
                def _(sy):
                    sy.wait_ge(s_ln2.h, t_i + 1)
                    ins = sy.dma_start(out=y[t_i * 128:(t_i + 1) * 128, :],
                                       in_=rln.ap()[:, par, :])
                    s_outp[t_i % 2].inc(ins, 16)

            emit_rd(0)
            emit_ln1_stats(0)
            emit_ln1_sqrt(0)
            emit_ln1_finqx(0)
            emit_xqT_evac(0, emit_xqT(0))

            for t_i in range(NQT):
                nxt = t_i + 1 if t_i + 1 < NQT else None
                if nxt is not None:
                    emit_rd(nxt)
                emit_gu(t_i, 0)
                emit_silu(t_i, 0)
                emit_hq(t_i, 0)
                emit_gu(t_i, 1)
                emit_silu(t_i, 1)
                ev0 = emit_hq_evac(t_i, 0, emit_hqT(t_i, 0))
                emit_down(t_i, 0, ev0)
                if nxt is not None:
                    emit_ln1_stats(nxt)
                    emit_ln1_sqrt(nxt)
                emit_hq(t_i, 1)
                emit_gu(t_i, 2)
                emit_silu(t_i, 2)
                ev1 = emit_hq_evac(t_i, 1, emit_hqT(t_i, 1))
                emit_down(t_i, 1, ev1)
                if nxt is not None:
                    emit_ln1_finqx(nxt)
                emit_hq(t_i, 2)
                emit_gu(t_i, 3)
                emit_silu(t_i, 3)
                ev2 = emit_hq_evac(t_i, 2, emit_hqT(t_i, 2))
                emit_down(t_i, 2, ev2)
                if nxt is not None:
                    emit_xqT_evac(nxt, emit_xqT(nxt))
                emit_hq(t_i, 3)
                emit_gu(t_i, 4)
                emit_silu(t_i, 4)
                ev3 = emit_hq_evac(t_i, 3, emit_hqT(t_i, 3))
                emit_down(t_i, 3, ev3)
                emit_hq(t_i, 4)
                emit_gu(t_i, 5)
                emit_silu(t_i, 5)
                ev4 = emit_hq_evac(t_i, 4, emit_hqT(t_i, 4))
                emit_down(t_i, 4, ev4)
                emit_hq(t_i, 5)
                ev5 = emit_hq_evac(t_i, 5, emit_hqT(t_i, 5))
                emit_down(t_i, 5, ev5)
                emit_tail(t_i)

            @blk.sync
            def _(sy):
                sy.wait_ge(s_outp[0].h, s_outp[0].n)
                sy.wait_ge(s_outp[1].h, s_outp[1].n)
                if DEBUG[0]:
                    for nm3, t3 in (("wgT", wgT), ("wdT", wdT), ("xqt", xqt),
                                    ("y1b", y1b), ("hqt", hqt), ("rlnv", rln)):
                        sy.dma_start(out=dbg[nm3], in_=t3.ap()).then_inc(
                            s_xr.h, 16)
                        s_xr.n += 16
                    sy.dma_start(out=dbg["st0"], in_=dbgsb.ap()).then_inc(
                        s_xr.h, 16)
                    s_xr.n += 16
                    sy.dma_start(out=dbg["p2"], in_=dbgp2.ap()).then_inc(
                        s_xr.h, 16)
                    s_xr.n += 16
                    for name, srcd in (("ktr", ktr_d), ("qtr", qtr_d),
                                       ("r1", r1_d), ("wgq", wgq_d)):
                        sy.dma_start(out=dbg[name], in_=srcd).then_inc(
                            s_xr.h, 16)
                        s_xr.n += 16
                    sy.wait_ge(s_xr.h, s_xr.n)

    return nc


_NC_CACHE = [None]


def kernel(L0_attn_x, L0_attn_w_q, L0_attn_w_kv, L0_attn_w_out, L0_attn_b_out,
           L0_attn_scale, L0_ffn_w_gate, L0_ffn_w_up, L0_ffn_w_down):
    xin = np.ascontiguousarray(np.asarray(L0_attn_x, dtype=np.float32))
    w_q = np.ascontiguousarray(np.asarray(L0_attn_w_q, dtype=np.float32))
    w_kv = np.ascontiguousarray(np.asarray(L0_attn_w_kv, dtype=np.float32))
    w_out = np.ascontiguousarray(np.asarray(L0_attn_w_out, dtype=np.float32))
    b_out = np.asarray(L0_attn_b_out, dtype=np.float32).reshape(1, D)
    scal = np.asarray(L0_attn_scale, dtype=np.float32).reshape(1, H)
    wg = np.ascontiguousarray(np.asarray(L0_ffn_w_gate, dtype=np.float32))
    wu = np.ascontiguousarray(np.asarray(L0_ffn_w_up, dtype=np.float32))
    wd = np.ascontiguousarray(np.asarray(L0_ffn_w_down, dtype=np.float32))
    eyev = np.eye(128, dtype=np.float32)

    if _NC_CACHE[0] is None:
        _NC_CACHE[0] = build_core()
    nc = _NC_CACHE[0]

    in_maps = []
    for c in range(8):
        b, qh = c // 2, c % 2
        xc = np.ascontiguousarray(np.roll(xin[b], -qh * QTOK, axis=0))
        in_maps.append({
            "x": xc, "w_q": w_q, "w_kv": w_kv, "w_out": w_out,
            "b_out": b_out, "scal": scal, "wg": wg, "wu": wu, "wd": wd,
            "eye": eyev,
        })

    res = run_bass_kernel_spmd(nc, in_maps, core_ids=list(range(8)))
    out = np.empty((4, SEQ, D), dtype=np.float32)
    for c in range(8):
        b, qh = c // 2, c % 2
        out[b, qh * QTOK:(qh + 1) * QTOK] = res.results[c]["y"]
    return out


# version 16
# speedup vs baseline: 1.1054x; 1.1054x over previous
import numpy as np
import concourse.bass as bass
import concourse.mybir as mybir
from concourse.bass_utils import run_bass_kernel_spmd
from contextlib import ExitStack

F32 = mybir.dt.float32
BF16 = mybir.dt.bfloat16
ALU = mybir.AluOpType
ACT = mybir.ActivationFunctionType
AX = mybir.AxisListType.X

D = 1024
H = 16
DH = 64
NPAIR = 8
SEQ = 4096
QTOK = 2048
NKT = SEQ // 128
NQT = QTOK // 128
QS = 1024
NQS = QTOK // QS
NTQ = QS // 128
IFF = 2816
GB = 16
MID2 = [0.0625, 0.5625, 1.5625, 3.0625, 6.25, 12.25, 25.0]
WQ_UNITS = (
    [("wg", i, 0, 1024) for i in range(22)]
    + [("wu", i, 0, 1024) for i in range(22)]
    + [("wd", i, s * 1024, 1024 if s < 2 else 768) for i in range(8) for s in range(3)]
)
N_WQ = len(WQ_UNITS)
DEBUG = [False]
WQ_UNIT_IDX = {(nm, ti, co): i for i, (nm, ti, co, _w) in enumerate(WQ_UNITS)}


class Sem:
    REGISTRY = None

    def __init__(self, stack, nc, name):
        self.h = stack.enter_context(nc.semaphore(name))
        self.n = 0
        if Sem.REGISTRY is not None:
            Sem.REGISTRY.append(self.h)

    def inc(self, instr, k=1):
        self.n += k
        instr.then_inc(self.h, k)
        return self.n


def build_core():
    nc = bass.Bass("TRN2", target_bir_lowering=False,
                   detect_race_conditions=False)

    x = nc.dram_tensor("x", [SEQ, D], F32, kind="ExternalInput").ap()
    w_q = nc.dram_tensor("w_q", [D, D], F32, kind="ExternalInput").ap()
    w_kv = nc.dram_tensor("w_kv", [D, 2 * D], F32, kind="ExternalInput").ap()
    w_out = nc.dram_tensor("w_out", [D, D], F32, kind="ExternalInput").ap()
    b_out = nc.dram_tensor("b_out", [1, D], F32, kind="ExternalInput").ap()
    scal = nc.dram_tensor("scal", [1, H], F32, kind="ExternalInput").ap()
    wgi = nc.dram_tensor("wg", [IFF, D], F32, kind="ExternalInput").ap()
    wui = nc.dram_tensor("wu", [IFF, D], F32, kind="ExternalInput").ap()
    wdi = nc.dram_tensor("wd", [D, IFF], F32, kind="ExternalInput").ap()
    eye = nc.dram_tensor("eye", [128, 128], F32, kind="ExternalInput").ap()
    y = nc.dram_tensor("y", [QTOK, D], F32, kind="ExternalOutput").ap()

    ktr_d = nc.dram_tensor("ktr_d", [NPAIR, 128, SEQ], BF16, kind="Internal").ap()
    qtr_d = nc.dram_tensor("qtr_d", [NPAIR, 128, QTOK], BF16, kind="Internal").ap()
    r1_d = nc.dram_tensor("r1_d", [QTOK, D], F32, kind="Internal").ap()
    wgq_d = nc.dram_tensor("wgq_d", [IFF, D], BF16, kind="Internal").ap()
    wuq_d = nc.dram_tensor("wuq_d", [IFF, D], BF16, kind="Internal").ap()
    wdq_d = nc.dram_tensor("wdq_d", [D, IFF], BF16, kind="Internal").ap()
    wq_in_d = {"wg": wgi, "wu": wui, "wd": wdi}
    wq_out_d = {"wg": wgq_d, "wu": wuq_d, "wd": wdq_d}
    dbg = {}
    if DEBUG[0]:
        dbg["ktr"] = nc.dram_tensor("dbg_ktr", [NPAIR, 128, SEQ], BF16,
                                    kind="ExternalOutput").ap()
        dbg["qtr"] = nc.dram_tensor("dbg_qtr", [NPAIR, 128, QTOK], BF16,
                                    kind="ExternalOutput").ap()
        dbg["r1"] = nc.dram_tensor("dbg_r1", [QTOK, D], F32,
                                   kind="ExternalOutput").ap()
        dbg["wgq"] = nc.dram_tensor("dbg_wgq", [IFF, D], BF16,
                                    kind="ExternalOutput").ap()
        dbg["v"] = nc.dram_tensor("dbg_v", [128, NKT, H, 65], BF16,
                                  kind="ExternalOutput").ap()
        dbg["kst"] = nc.dram_tensor("dbg_kst", [128, 4, H], F32,
                                    kind="ExternalOutput").ap()
        dbg["khat"] = nc.dram_tensor("dbg_khat", [128, 2, D], BF16,
                                     kind="ExternalOutput").ap()
        dbg["ktr01"] = nc.dram_tensor("dbg_ktr01", [2, 128, 8, 128], BF16,
                                      kind="ExternalOutput").ap()
        dbg["st0"] = nc.dram_tensor("dbg_st0", [128, 8, 64], F32,
                                    kind="ExternalOutput").ap()
        dbg["p2"] = nc.dram_tensor("dbg_p2", [128, 6, 512], BF16,
                                   kind="ExternalOutput").ap()
        dbg["wgT"] = nc.dram_tensor("dbg_wgT", [128, 8, IFF], BF16,
                                    kind="ExternalOutput").ap()
        dbg["wdT"] = nc.dram_tensor("dbg_wdT", [128, 22, D], BF16,
                                    kind="ExternalOutput").ap()
        dbg["xqt"] = nc.dram_tensor("dbg_xqt", [128, 2, 8, 128], BF16,
                                    kind="ExternalOutput").ap()
        dbg["y1b"] = nc.dram_tensor("dbg_y1b", [128, 2, D], F32,
                                    kind="ExternalOutput").ap()
        dbg["hqt"] = nc.dram_tensor("dbg_hqt", [128, 1, 4, 128], BF16,
                                    kind="ExternalOutput").ap()
        dbg["rlnv"] = nc.dram_tensor("dbg_rlnv", [128, 2, D], F32,
                                     kind="ExternalOutput").ap()

    st = ExitStack()
    with st:
        e = st.enter_context
        Sem.REGISTRY = []

        def S(name):
            return Sem(st, nc, name)

        eye_sb = e(nc.sbuf_tensor("eye_sb", [128, 128], BF16))
        ebc = e(nc.sbuf_tensor("ebc", [128, H], F32))
        negm = e(nc.sbuf_tensor("negm", [128, H], F32))
        negmsh = e(nc.sbuf_tensor("negmsh", [128, 8], F32))
        bout_bc = e(nc.sbuf_tensor("bout_bc", [128, D], F32))
        qbA = e(nc.sbuf_tensor("qbA", [128, 1024], F32))
        qh1 = e(nc.sbuf_tensor("qh1", [128, 1024], BF16))
        qh2 = e(nc.sbuf_tensor("qh2", [128, 1024], BF16))
        qh3 = e(nc.sbuf_tensor("qh3", [128, 1024], BF16))
        qh4 = e(nc.sbuf_tensor("qh4", [128, 1024], BF16))
        qg1 = e(nc.sbuf_tensor("qg1", [128, 64], F32))
        qg2 = e(nc.sbuf_tensor("qg2", [128, 64], F32))
        wstg = e(nc.sbuf_tensor("wstg", [128, 2, 1024], F32))
        wqo = e(nc.sbuf_tensor("wqo", [128, 2, 1024], BF16))
        lnm = e(nc.sbuf_tensor("lnm", [128, 8], F32))
        bst = e(nc.sbuf_tensor("bst", [128, 2, 6], F32))
        bagg = e(nc.sbuf_tensor("bagg", [128, 2], F32))
        rtw = e(nc.sbuf_tensor("rtw", [128, 8], F32))
        rsw = e(nc.sbuf_tensor("rsw", [128, 8], F32))
        dbgsb = e(nc.sbuf_tensor("dbgsb", [128, 8, 64], F32))
        dbgp2 = e(nc.sbuf_tensor("dbgp2", [128, 6, 512], BF16))
        lnx = e(nc.sbuf_tensor("lnx", [128, 1024], F32))

        vself = S("vself")
        sself = S("sself")
        s_misc = S("s_misc")
        s_set = S("s_set")
        p1bar = S("p1bar")
        p2bar = S("p2bar")
        s_x = S("s_x")
        s_xp = [S("s_xp0"), S("s_xp1")]
        s_w = S("s_w")
        s_w2 = S("s_w2")
        s_xt = S("s_xt")
        s_xte = S("s_xte")
        s_kv = S("s_kv")
        s_qm = S("s_qm")
        s_sq = S("s_sq")
        s_vst = S("s_vst")
        s_ssq = S("s_ssq")
        s_nrm = S("s_nrm")
        s_ktw = S("s_ktw")
        s_qtw = S("s_qtw")
        s_ktwp = [S("s_ktwp0"), S("s_ktwp1")]
        s_qtwp = [S("s_qtwp0"), S("s_qtwp1")]
        s_wfi = S("s_wfi")
        s_wfq = S("s_wfq")
        s_wfw = S("s_wfw")
        s_wfip = [S("s_wfip0"), S("s_wfip1")]
        s_wfwp = [S("s_wfwp0"), S("s_wfwp1")]
        s_kin = S("s_kin")
        s_qin = S("s_qin")
        s_kinp = [S("s_kinp0"), S("s_kinp1")]
        s_qinp = [S("s_qinp0"), S("s_qinp1")]
        s_simA = S("s_simA")
        s_simB = S("s_simB")
        s_expA = S("s_expA")
        s_expB = S("s_expB")
        s_avA = S("s_avA")
        s_avB = S("s_avB")
        s_oev = S("s_oev")
        s_otr = S("s_otr")
        s_ofin = S("s_ofin")
        s_obt = S("s_obt")
        s_obe = S("s_obe")
        s_op = S("s_op")
        s_xr = S("s_xr")
        s_xrp = [S("s_xrp0"), S("s_xrp1")]
        s_r1 = S("s_r1")
        s_r1w = S("s_r1w")
        s_r1wp = [S("s_r1wp0"), S("s_r1wp1")]
        s_r1i = S("s_r1i")
        s_r1ip = [S("s_r1ip0"), S("s_r1ip1")]
        s_wqi = S("s_wqi")
        s_trp = S("s_trp")
        s_tre = S("s_tre")
        s_xq3 = S("s_xq3")
        s_gu = S("s_gu")
        s_sil = S("s_sil")
        s_h = S("s_h")
        s_hq = S("s_hq")
        s_dn = S("s_dn")
        s_y2e = S("s_y2e")
        s_ln2 = S("s_ln2")
        s_out = S("s_out")
        s_outp = [S("s_outp0"), S("s_outp1")]

        with nc.Block() as blk0:
            @blk0.sync
            def _(sy):
                for hh_ in Sem.REGISTRY:
                    sy.sem_clear(hh_)

        blk = e(nc.Block())

        def vedge(vec):
            c = vself.inc(vec.engine_nop())
            vec.wait_ge(vself.h, c)

        def vrecip(vec, out, in_):
            vedge(vec)
            vec.reciprocal(out, in_)
            vedge(vec)

        def quant_slab(vec, src_ap, dst_ap, width):
            G = width // GB
            a = qbA.ap()[:, 0:width]
            sq = qh1.ap()[:, 0:width]
            b = qh2.ap()[:, 0:width]
            c = qh3.ap()[:, 0:width]
            d = qh4.ap()[:, 0:width]
            g1 = qg1.ap()[:, 0:G]
            g2 = qg2.ap()[:, 0:G]
            srcg = src_ap.rearrange("p (g s) -> p g s", s=GB)
            vec.tensor_reduce(g1, srcg, AX, ALU.max, apply_absolute_value=True)
            vedge(vec)
            vec.tensor_scalar(g1, g1, 1.0 / 6.0, 1e-12, ALU.mult, ALU.max)
            vrecip(vec, g2, g1)
            vec.tensor_tensor(g2, g2, g2, ALU.mult)
            vedge(vec)
            g2b = g2.unsqueeze(2).broadcast_to([128, G, GB])
            g1b = g1.unsqueeze(2).broadcast_to([128, G, GB])
            ag = a.rearrange("p (g s) -> p g s", s=GB)
            sqg = sq.rearrange("p (g s) -> p g s", s=GB)
            vec.tensor_tensor(a, src_ap, src_ap, ALU.mult)
            vec.tensor_tensor(ag, ag, g2b, ALU.mult)
            vec.tensor_scalar(b, a, MID2[0], None, ALU.is_gt)
            for m2 in MID2[1:]:
                vec.scalar_tensor_tensor(b, a, m2, b, ALU.is_gt, ALU.add)
            vec.tensor_scalar(c, b, 4.0, 0.0, ALU.subtract, ALU.max)
            vec.tensor_scalar(d, b, 6.0, 0.0, ALU.subtract, ALU.max)
            vec.scalar_tensor_tensor(c, d, 2.0, c, ALU.mult, ALU.add)
            vec.tensor_tensor(c, c, b, ALU.add)
            vec.tensor_scalar(a, src_ap, 0.0, None, ALU.is_ge)
            vec.scalar_tensor_tensor(ag, ag, 0.5, g1b, ALU.subtract, ALU.mult)
            return vec.tensor_tensor(dst_ap, c, a, ALU.mult)

        wq_next = [0]

        def emit_wq_unit():
            u = wq_next[0]
            if u >= N_WQ:
                return
            wq_next[0] = u + 1
            name, ti, co, w = WQ_UNITS[u]
            par = u % 2

            @blk.sync
            def _(sy):
                if u >= 2:
                    sy.wait_ge(s_wfq.h, u - 1)
                ins = sy.dma_start(
                    out=wstg.ap()[:, par, 0:w],
                    in_=wq_in_d[name][ti * 128:(ti + 1) * 128, co:co + w])
                s_wfip[u % 2].inc(ins, 16)

            @blk.vector
            def _(vec):
                vec.wait_ge(s_wfip[u % 2].h, 16 * (u // 2 + 1))
                if u >= 2:
                    vec.wait_ge(s_wfwp[u % 2].h, 16 * (u // 2))
                ins = quant_slab(vec, wstg.ap()[:, par, 0:w],
                                 wqo.ap()[:, par, 0:w], w)
                s_wfq.inc(ins)

            @blk.sync
            def _(sy):
                sy.wait_ge(s_wfq.h, u + 1)
                ins = sy.dma_start(
                    out=wq_out_d[name][ti * 128:(ti + 1) * 128, co:co + w],
                    in_=wqo.ap()[:, par, 0:w])
                s_wfwp[u % 2].inc(ins, 16)

        def ln_stats(vec, src_ap):
            vec.bn_stats(bst.ap()[:, 0, :], src_ap[:, 0:512])
            vec.bn_stats(bst.ap()[:, 1, :], src_ap[:, 512:1024])
            vedge(vec)
            vec.bn_aggr(bagg.ap(), bst.ap().rearrange("p a b -> p (a b)"))
            vedge(vec)
            ins = vec.tensor_scalar(
                rtw.ap(), bagg.ap()[:, 1:2].broadcast_to([128, 8]),
                1.0, 1e-5, ALU.mult, ALU.add)
            return vself.inc(ins)

        def ln_sqrt(sc, wait_c):
            sc.wait_ge(vself.h, wait_c)
            sc.activation(rsw.ap(), rtw.ap(), ACT.Ln)
            return sself.inc(sc.activation(rsw.ap(), rsw.ap(), ACT.Exp,
                                           scale=-0.5))

        def ln_fin(vec, src_ap, dst_ap, wait_s):
            vec.wait_ge(sself.h, wait_s)
            vec.tensor_scalar(lnx.ap(), src_ap, bagg.ap()[:, 0:1],
                              None, ALU.subtract)
            return vec.tensor_scalar(dst_ap, lnx.ap(), rsw.ap()[:, 0:1],
                                     None, ALU.mult)

        @blk.gpsimd
        def _(gp):
            s_misc.inc(gp.dma_start(out=eye_sb.ap(), in_=eye), 16)
            s_misc.inc(gp.dma_start(
                out=ebc.ap(), in_=scal.partition_broadcast(128).squeeze(1)), 16)
            s_misc.inc(gp.dma_start(
                out=bout_bc.ap(), in_=b_out.partition_broadcast(128).squeeze(1)), 16)

        @blk.scalar
        def _(sc):
            sc.wait_ge(s_misc.h, s_misc.n)
            ins = sc.activation(ebc.ap(), ebc.ap(), ACT.Exp)
            sself.inc(ins)

        @blk.vector
        def _(vec):
            if DEBUG[0]:
                vec.memset(dbgsb.ap(), 0.0)
                vec.memset(dbgp2.ap(), 0.0)
            vec.wait_ge(sself.h, sself.n)
            vec.tensor_scalar(negm.ap(), ebc.ap(), 80.0, -1.0,
                              ALU.min, ALU.mult)
            vec.tensor_tensor(lnm.ap()[:, 0:8], ebc.ap()[:, 0:8],
                              ebc.ap()[:, 8:16], ALU.max)
            vedge(vec)
            vec.tensor_tensor(lnm.ap()[:, 0:4], lnm.ap()[:, 0:4],
                              lnm.ap()[:, 4:8], ALU.max)
            vedge(vec)
            vec.tensor_tensor(lnm.ap()[:, 0:2], lnm.ap()[:, 0:2],
                              lnm.ap()[:, 2:4], ALU.max)
            vedge(vec)
            vec.tensor_tensor(lnm.ap()[:, 0:1], lnm.ap()[:, 0:1],
                              lnm.ap()[:, 1:2], ALU.max)
            vedge(vec)
            ins = vec.tensor_scalar(negmsh.ap(), 
                                    lnm.ap()[:, 0:1].broadcast_to([128, 8]),
                                    80.0, -1.0, ALU.min, ALU.mult)
            s_set.inc(ins)

        SETUP = s_set.n

        with ExitStack() as p12s:
            e12 = p12s.enter_context
            v_sb = e12(nc.sbuf_tensor("v_sb", [128, NKT, H, 65], BF16))

            with ExitStack() as p1s:
                e1 = p1s.enter_context
                xb = e1(nc.sbuf_tensor("xb", [128, 2, D], BF16))
                xt_sb = e1(nc.sbuf_tensor("xt_sb", [128, 2, 8, 128], BF16))
                wkv_sb = e1(nc.sbuf_tensor("wkv_sb", [128, 8, 2 * D], BF16))
                wqp_sb = e1(nc.sbuf_tensor("wqp_sb", [128, 8, D], BF16))
                ksq = e1(nc.sbuf_tensor("ksq", [128, D], F32))
                qsq = e1(nc.sbuf_tensor("qsq", [128, D], F32))
                kst = e1(nc.sbuf_tensor("kst", [128, 4, H], F32))
                qst = e1(nc.sbuf_tensor("qst", [128, 4, H], F32))
                khat = e1(nc.sbuf_tensor("khat", [128, 2, D], BF16))
                qhat = e1(nc.sbuf_tensor("qhat", [128, 2, D], BF16))
                ktr = e1(nc.sbuf_tensor("ktr", [128, 2, 8, 128], BF16))
                qtr = e1(nc.sbuf_tensor("qtr", [128, 2, 8, 128], BF16))
                ptx = e1(nc.psum_tensor("ptx", [128, 1024], F32))
                kv_ps = e1(nc.psum_tensor("kv_ps", [128, 2048], F32))
                q_ps = e1(nc.psum_tensor("q_ps", [128, 1024], F32))

                @blk.gpsimd
                def _(gp):
                    for cch in range(8):
                        s_w.inc(gp.dma_start(
                            out=wkv_sb.ap()[:, cch, :],
                            in_=w_kv[cch * 128:(cch + 1) * 128, :]), 16)
                        s_w.inc(gp.dma_start(
                            out=wqp_sb.ap()[:, cch, :],
                            in_=w_q[cch * 128:(cch + 1) * 128, :]), 16)

                @blk.vector
                def _(vec):
                    ins = vec.memset(v_sb.ap()[:, :, :, 64], 1.0)
                    s_set.inc(ins)
                P1SET = s_set.n

                grp = [0]

                def pe_tgroup(t, srcs):
                    g = grp[0]
                    grp[0] = g + 1
                    half = g % 2
                    if g >= 2:
                        t.wait_ge(s_xte.h, g - 1)
                    ins = None
                    for j, ap_ in enumerate(srcs):
                        ins = t.matmul(
                            ptx.ap()[:, half * 512 + j * 128:
                                     half * 512 + j * 128 + 128],
                            ap_, eye_sb.ap(), start=True, stop=True)
                    s_xt.inc(ins)
                    return g

                def vec_egroup(vec, g, dst_ap):
                    vec.wait_ge(s_xt.h, g + 1)
                    half = g % 2
                    src = ptx.ap()[:, half * 512:(half + 1) * 512].rearrange(
                        "p (c f) -> p c f", f=128)
                    ins = vec.tensor_copy(dst_ap, src)
                    s_xte.inc(ins)

                def gbase(kt):
                    return 6 * min(kt, NQT) + 4 * max(0, kt - NQT)

                for kt in range(NKT):
                    par = kt % 2
                    isq = kt < NQT
                    CQ = min(kt, NQT)
                    sq_base = kt + CQ
                    nrm_base = 2 * kt + CQ
                    gb = gbase(kt)
                    assert grp[0] == gb, (kt, grp[0], gb)

                    @blk.gpsimd
                    def _(gp):
                        if kt >= 2:
                            gp.wait_ge(s_xt.h, gbase(kt - 2) + 2)
                        ins = gp.dma_start(out=xb.ap()[:, par, :],
                                           in_=x[kt * 128:(kt + 1) * 128, :])
                        s_xp[kt % 2].inc(ins, 16)

                    @blk.tensor
                    def _(t):
                        if kt == 0:
                            t.wait_ge(s_misc.h, s_misc.n)
                            t.wait_ge(s_w.h, s_w.n)
                            t.wait_ge(s_set.h, P1SET)
                        t.wait_ge(s_xp[kt % 2].h, 16 * (kt // 2 + 1))
                        for g in range(2):
                            srcs = [xb.ap()[:, par,
                                            (g * 4 + j) * 128:(g * 4 + j + 1) * 128]
                                    for j in range(4)]
                            pe_tgroup(t, srcs)
                        t.wait_ge(s_xte.h, gb + 2)
                        if kt >= 1:
                            pb = 2 * (kt - 1) + min(kt - 1, NQT)
                            t.wait_ge(s_nrm.h, pb + 2)
                            t.wait_ge(s_sq.h, (kt - 1) + min(kt - 1, NQT) + 1)
                        ins = None
                        for cch in range(8):
                            for sl in range(4):
                                ins = t.matmul(
                                    kv_ps.ap()[:, sl * 512:(sl + 1) * 512],
                                    xt_sb.ap()[:, par, cch, :],
                                    wkv_sb.ap()[:, cch, sl * 512:(sl + 1) * 512],
                                    start=(cch == 0), stop=(cch == 7))
                        s_kv.inc(ins)
                        if isq:
                            if kt >= 1:
                                pb = 2 * (kt - 1) + min(kt - 1, NQT)
                                t.wait_ge(s_nrm.h, pb + 3)
                                t.wait_ge(s_sq.h,
                                          (kt - 1) + min(kt - 1, NQT) + 2)
                            ins = None
                            for cch in range(8):
                                for sl in range(2):
                                    ins = t.matmul(
                                        q_ps.ap()[:, sl * 512:(sl + 1) * 512],
                                        xt_sb.ap()[:, par, cch, :],
                                        wqp_sb.ap()[:, cch,
                                                    sl * 512:(sl + 1) * 512],
                                        start=(cch == 0), stop=(cch == 7))
                            s_qm.inc(ins)

                    @blk.scalar
                    def _(sc):
                        sc.wait_ge(s_kv.h, kt + 1)
                        s_sq.inc(sc.activation(ksq.ap(), kv_ps.ap()[:, 0:D],
                                               ACT.Square))
                        if isq:
                            sc.wait_ge(s_qm.h, kt + 1)
                            s_sq.inc(sc.activation(qsq.ap(), q_ps.ap(),
                                                   ACT.Square))
                        sc.wait_ge(s_vst.h, sq_base + 1)
                        sc.activation(kst.ap()[:, 1, :],
                                      kst.ap()[:, 0, :], ACT.Ln)
                        s_ssq.inc(sc.activation(kst.ap()[:, 1, :],
                                                kst.ap()[:, 1, :], ACT.Exp,
                                                scale=0.5))
                        if isq:
                            sc.wait_ge(s_vst.h, sq_base + 2)
                            sc.activation(qst.ap()[:, 1, :],
                                          qst.ap()[:, 0, :], ACT.Ln)
                            s_ssq.inc(sc.activation(qst.ap()[:, 1, :],
                                                    qst.ap()[:, 1, :], ACT.Exp,
                                                    scale=0.5))

                    @blk.vector
                    def _(vec):
                        for g in range(2):
                            vec_egroup(vec, gb + g,
                                       xt_sb.ap()[:, par, g * 4:(g + 1) * 4, :])
                        vec.wait_ge(s_kv.h, kt + 1)
                        if DEBUG[0] >= 2 and kt == 0:
                            vec.tensor_copy(dbgsb.ap()[:, 0, :],
                                            kv_ps.ap()[:, 0:64])
                            vec.tensor_copy(dbgsb.ap()[:, 1, :],
                                            kv_ps.ap()[:, D:D + 64])
                        ins = vec.tensor_copy(
                            v_sb.ap()[:, kt, :, 0:64],
                            kv_ps.ap()[:, D:2 * D].rearrange(
                                "p (h d) -> p h d", d=DH))
                        s_nrm.inc(ins)
                        vec.wait_ge(s_sq.h, sq_base + 1)
                        if DEBUG[0] >= 2 and kt == 0:
                            vec.tensor_copy(dbgsb.ap()[:, 2, :],
                                            ksq.ap()[:, 0:64])
                        vec.tensor_reduce(
                            kst.ap()[:, 0, :],
                            ksq.ap().rearrange("p (h d) -> p h d", d=DH),
                            AX, ALU.add)
                        s_vst.inc(vec.engine_nop())
                        vec.wait_ge(s_ssq.h, sq_base + 1)
                        if DEBUG[0] >= 2 and kt == 0:
                            vec.tensor_copy(dbgsb.ap()[:, 3, 0:16],
                                            kst.ap()[:, 0, :])
                            vec.tensor_copy(dbgsb.ap()[:, 3, 16:32],
                                            kst.ap()[:, 1, :])
                        vec.tensor_scalar(kst.ap()[:, 2, :], kst.ap()[:, 1, :],
                                          1e-12, None, ALU.max)
                        vrecip(vec, kst.ap()[:, 3, :], kst.ap()[:, 2, :])
                        if DEBUG[0] >= 2 and kt == 0:
                            vec.tensor_copy(dbgsb.ap()[:, 3, 32:48],
                                            kst.ap()[:, 3, :])
                        if kt >= 2:
                            vec.wait_ge(s_xt.h, gbase(kt - 2) + 4)
                        rb = kst.ap()[:, 3, :].unsqueeze(2).broadcast_to(
                            [128, H, DH])
                        ins = vec.tensor_tensor(
                            khat.ap()[:, par, :].rearrange(
                                "p (h d) -> p h d", d=DH),
                            kv_ps.ap()[:, 0:D].rearrange(
                                "p (h d) -> p h d", d=DH),
                            rb, ALU.mult)
                        s_nrm.inc(ins)
                        if isq:
                            vec.wait_ge(s_sq.h, sq_base + 2)
                            vec.tensor_reduce(
                                qst.ap()[:, 0, :],
                                qsq.ap().rearrange("p (h d) -> p h d", d=DH),
                                AX, ALU.add)
                            s_vst.inc(vec.engine_nop())
                            vec.wait_ge(s_ssq.h, sq_base + 2)
                            vec.tensor_scalar(qst.ap()[:, 2, :],
                                              qst.ap()[:, 1, :],
                                              1e-12, None, ALU.max)
                            vrecip(vec, qst.ap()[:, 3, :], qst.ap()[:, 2, :])
                            vec.tensor_tensor(qst.ap()[:, 3, :],
                                              qst.ap()[:, 3, :],
                                              ebc.ap(), ALU.mult)
                            if kt >= 2:
                                vec.wait_ge(s_xt.h, gbase(kt - 2) + 6)
                            rqb = qst.ap()[:, 3, :].unsqueeze(2).broadcast_to(
                                [128, H, DH])
                            ins = vec.tensor_tensor(
                                qhat.ap()[:, par, :].rearrange(
                                    "p (h d) -> p h d", d=DH),
                                q_ps.ap().rearrange("p (h d) -> p h d", d=DH),
                                rqb, ALU.mult)
                            s_nrm.inc(ins)

                    @blk.tensor
                    def _(t):
                        t.wait_ge(s_nrm.h, nrm_base + 2)
                        for g in range(2):
                            srcs = [khat.ap()[:, par,
                                              (g * 4 + j) * 128:
                                              (g * 4 + j + 1) * 128]
                                    for j in range(4)]
                            pe_tgroup(t, srcs)
                        if isq:
                            t.wait_ge(s_nrm.h, nrm_base + 3)
                            for g in range(2):
                                srcs = [qhat.ap()[:, par,
                                                  (g * 4 + j) * 128:
                                                  (g * 4 + j + 1) * 128]
                                        for j in range(4)]
                                pe_tgroup(t, srcs)

                    @blk.vector
                    def _(vec):
                        if kt >= 2:
                            vec.wait_ge(s_ktwp[kt % 2].h, 32 * (kt // 2))
                        if DEBUG[0] >= 2 and kt == 0:
                            vec.tensor_copy(dbgsb.ap()[:, 4, :],
                                            khat.ap()[:, par, 0:64])
                        for g in range(2):
                            vec_egroup(vec, gb + 2 + g,
                                       ktr.ap()[:, par, g * 4:(g + 1) * 4, :])
                        if DEBUG[0] >= 2 and kt == 0:
                            vec.tensor_copy(dbgsb.ap()[:, 5, :],
                                            ktr.ap()[:, par, 0, 0:64])
                        if isq:
                            if kt >= 2:
                                vec.wait_ge(s_qtwp[kt % 2].h, 32 * (kt // 2))
                            for g in range(2):
                                vec_egroup(vec, gb + 4 + g,
                                           qtr.ap()[:, par,
                                                    g * 4:(g + 1) * 4, :])

                    @blk.sync
                    def _(sy):
                        sy.wait_ge(s_xte.h, gb + 4)
                        if DEBUG[0] >= 2 and kt < 2:
                            sy.dma_start(out=dbg["ktr01"][kt, :, :, :],
                                         in_=ktr.ap()[:, par, :, :]).then_inc(
                                s_xr.h, 16)
                            s_xr.n += 16
                        for hseg in range(2):
                            src = ktr.ap()[:, par, hseg * 4:(hseg + 1) * 4, :]
                            dst = ktr_d.transpose([1, 0, 2])[
                                :, hseg * 4:(hseg + 1) * 4,
                                kt * 128:(kt + 1) * 128]
                            s_ktwp[kt % 2].inc(sy.dma_start(out=dst, in_=src), 16)
                        if isq:
                            sy.wait_ge(s_xte.h, gb + 6)
                            for hseg in range(2):
                                src = qtr.ap()[:, par,
                                               hseg * 4:(hseg + 1) * 4, :]
                                dst = qtr_d.transpose([1, 0, 2])[
                                    :, hseg * 4:(hseg + 1) * 4,
                                    kt * 128:(kt + 1) * 128]
                                s_qtwp[kt % 2].inc(sy.dma_start(out=dst, in_=src), 16)

                    pass

                @blk.vector
                def _(vec):
                    p1bar.inc(vec.engine_nop())

                @blk.scalar
                def _(sc):
                    p1bar.inc(sc.nop())

                @blk.sync
                def _(sy):
                    sy.wait_ge(s_ktwp[0].h, s_ktwp[0].n)
                    sy.wait_ge(s_ktwp[1].h, s_ktwp[1].n)
                    sy.wait_ge(s_qtwp[0].h, s_qtwp[0].n)
                    sy.wait_ge(s_qtwp[1].h, s_qtwp[1].n)
                    if DEBUG[0]:
                        sy.wait_ge(s_nrm.h, s_nrm.n)
                        for nm2, t2 in (("v", v_sb), ("kst", kst),
                                        ("khat", khat)):
                            sy.dma_start(out=dbg[nm2], in_=t2.ap()).then_inc(
                                s_xr.h, 16)
                            s_xr.n += 16
                        sy.wait_ge(s_xr.h, s_xr.n)
                    p1bar.inc(sy.nop())

                @blk.gpsimd
                def _(gp):
                    gp.wait_ge(s_xp[0].h, s_xp[0].n)
                    gp.wait_ge(s_xp[1].h, s_xp[1].n)
                    gp.wait_ge(s_w.h, s_w.n)
                    gp.wait_ge(s_misc.h, s_misc.n)
                    p1bar.inc(gp.engine_nop())

            QS2 = 512
            NQS2 = QTOK // QS2
            NT2 = QS2 // 128
            with ExitStack() as p2s:
                e2 = p2s.enter_context
                kpair = e2(nc.sbuf_tensor("kpair", [128, 2, SEQ], BF16))
                qpair = e2(nc.sbuf_tensor("qpair", [128, 2, QS2], BF16))
                ptile = e2(nc.sbuf_tensor("ptile", [128, 3, 2, QS2], BF16))
                ofm = e2(nc.sbuf_tensor("ofm", [65, 2, QS2], BF16))
                orc = e2(nc.sbuf_tensor("orc", [128, 1], F32))
                otok = e2(nc.sbuf_tensor("otok", [128, NT2, H, 64], BF16))
                wout_sb = e2(nc.sbuf_tensor("wout_sb", [128, 8, D], BF16))
                xresid = e2(nc.sbuf_tensor("xresid", [128, 2, D], F32))
                r1b = e2(nc.sbuf_tensor("r1b", [128, 2, D], F32))
                sAB = e2(nc.psum_tensor("sAB", [128, 2, 2, 512], F32))
                oA = e2(nc.psum_tensor("oA", [128, 512], F32))
                oB = e2(nc.psum_tensor("oB", [128, 512], F32))
                tailp = e2(nc.psum_tensor("tailp", [128, 1024], F32))
                oX = {0: oA, 1: oB}

                @blk.gpsimd
                def _(gp):
                    gp.wait_ge(p1bar.h, 4)
                    for cch in range(8):
                        s_w2.inc(gp.dma_start(
                            out=wout_sb.ap()[:, cch, :],
                            in_=w_out[cch * 128:(cch + 1) * 128, :]), 16)
                    p2bar.inc(gp.engine_nop())

                first = {"t": True, "sc": True, "sy": True, "v": True}

                def emit_kq_load(PIj):
                    qsj, pairj = divmod(PIj, NPAIR)
                    parj = PIj % 2

                    @blk.sync
                    def _(sy):
                        if first["sy"]:
                            sy.wait_ge(p1bar.h, 4)
                            first["sy"] = False
                        if PIj >= 2:
                            sy.wait_ge(s_simA.h, 32 * (PIj - 1))
                        ins = sy.dma_start(out=kpair.ap()[:, parj, :],
                                           in_=ktr_d[pairj, :, :])
                        s_kinp[parj].inc(ins, 16)
                        ins = sy.dma_start(
                            out=qpair.ap()[:, parj, :],
                            in_=qtr_d[pairj, :, qsj * QS2:(qsj + 1) * QS2])
                        s_qinp[parj].inc(ins, 16)

                emit_kq_load(0)
                for qs in range(NQS2):
                    for pair in range(NPAIR):
                        PI = qs * NPAIR + pair
                        par = PI % 2

                        @blk.tensor
                        def _(t):
                            if first["t"]:
                                t.wait_ge(p1bar.h, 4)
                                first["t"] = False
                            t.wait_ge(s_kinp[PI % 2].h, 16 * (PI // 2 + 1))
                            t.wait_ge(s_qinp[PI % 2].h, 16 * (PI // 2 + 1))
                            if pair == 0 and qs > 0:
                                t.wait_ge(s_r1.h, NT2 * qs)
                                t.wait_ge(s_obe.h, 2 * NT2 * qs)

                            def emit_sims(kt):
                                if kt >= 2:
                                    t.wait_ge(s_expA.h, PI * 32 + kt - 1)
                                elif PI > 0:
                                    t.wait_ge(s_expA.h, PI * 32)
                                ins = None
                                for hh in range(2):
                                    ins = t.matmul(
                                        sAB.ap()[:, kt % 2, hh, :],
                                        kpair.ap()[hh * 64:(hh + 1) * 64, par,
                                                   kt * 128:(kt + 1) * 128],
                                        qpair.ap()[hh * 64:(hh + 1) * 64,
                                                   par, :],
                                        start=True, stop=True,
                                        tile_position=(hh * 64, 0))
                                s_simA.inc(ins)

                            def emit_avs(kt):
                                t.wait_ge(s_expA.h, PI * 32 + kt + 1)
                                for hh in range(2):
                                    head = pair * 2 + hh
                                    if kt == 0 and PI >= 1:
                                        t.wait_ge(
                                            s_ofin.h,
                                            NT2 * (2 * (PI - 1) + hh + 1))
                                    ins = t.matmul(
                                        oX[hh].ap()[0:65, :],
                                        v_sb.ap()[:, kt, head, :],
                                        ptile.ap()[:, kt % 3, hh, :],
                                        start=(kt == 0), stop=(kt == NKT - 1))
                                    if hh == 1:
                                        s_avA.inc(ins)

                            for kt in range(NKT):
                                emit_sims(kt)
                                if kt >= 2:
                                    emit_avs(kt - 2)
                            emit_avs(NKT - 2)
                            emit_avs(NKT - 1)

                        @blk.scalar
                        def _(sc):
                            if first["sc"]:
                                sc.wait_ge(p1bar.h, 4)
                                first["sc"] = False
                            for kt in range(NKT):
                                sc.wait_ge(s_simA.h, PI * 32 + kt + 1)
                                if kt >= 3:
                                    sc.wait_ge(s_avA.h, PI * 32 + kt - 2)
                                ins = sc.activation(
                                    ptile.ap()[:, kt % 3, :, :].rearrange(
                                        "p a b -> p (a b)"),
                                    sAB.ap()[:, kt % 2, :, :].rearrange(
                                        "p a b -> p (a b)"),
                                    ACT.Exp,
                                    bias=negmsh.ap()[:, 0:1])
                                s_expA.inc(ins)

                        for hh in range(2):
                            head = pair * 2 + hh
                            obase = (2 * PI + hh) * NT2

                            @blk.vector
                            def _(vec):
                                if first["v"]:
                                    vec.wait_ge(p1bar.h, 4)
                                    first["v"] = False
                                vec.wait_ge(s_avA.h, (PI + 1) * 32)
                                if PI >= 1:
                                    vec.wait_ge(s_otr.h,
                                                (2 * (PI - 1) + hh + 1) * NT2)
                                ins = vec.tensor_copy(ofm.ap()[0:65, hh, :],
                                                      oX[hh].ap()[0:65, :])
                                s_oev.inc(ins)

                            @blk.tensor
                            def _(t):
                                t.wait_ge(s_oev.h, 2 * PI + hh + 1)
                                if PI == 0 and hh == 0 and qs == 0:
                                    pass
                                elif hh == 0 and pair == 0:
                                    t.wait_ge(s_obe.h, 2 * NT2 * qs)
                                for tq in range(NT2):
                                    gidx = obase + tq
                                    if gidx >= 2:
                                        t.wait_ge(s_ofin.h, gidx - 1)
                                    ins = t.matmul(
                                        tailp.ap()[:, (gidx % 2) * 512:
                                                   (gidx % 2) * 512 + 65],
                                        ofm.ap()[0:65, hh,
                                                 tq * 128:(tq + 1) * 128],
                                        eye_sb.ap()[0:65, 0:65],
                                        start=True, stop=True)
                                    s_otr.inc(ins)

                            @blk.vector
                            def _(vec):
                                for tq in range(NT2):
                                    vec.wait_ge(s_otr.h, obase + tq + 1)
                                    if qs > 0:
                                        vec.wait_ge(s_op.h, NT2 * qs)
                                    base = ((obase + tq) % 2) * 512
                                    vrecip(vec,
                                           orc.ap(),
                                           tailp.ap()[:, base + 64:base + 65])
                                    vedge(vec)
                                    ins = vec.tensor_scalar(
                                        otok.ap()[:, tq, head, :],
                                        tailp.ap()[:, base:base + 64],
                                        orc.ap(), None, ALU.mult)
                                    s_ofin.inc(ins)

                        if PI + 1 < NQS2 * NPAIR:
                            emit_kq_load(PI + 1)
                        emit_wq_unit()
                        emit_wq_unit()

                    for tq in range(NT2):
                        R = qs * NT2 + tq
                        par = R % 2

                        @blk.sync
                        def _(sy):
                            if R >= 2:
                                sy.wait_ge(s_r1.h, R - 1)
                            ins = sy.dma_start(out=xresid.ap()[:, par, :],
                                               in_=x[R * 128:(R + 1) * 128, :])
                            s_xrp[R % 2].inc(ins, 16)

                        og0 = (qs * NT2 + tq) * 2

                        @blk.tensor
                        def _(t):
                            if tq == 0:
                                t.wait_ge(s_ofin.h, (qs + 1) * 16 * NT2)
                                t.wait_ge(s_expA.h, (qs * NPAIR + NPAIR) * 32)
                            for g in range(2):
                                og = og0 + g
                                if og >= 2:
                                    t.wait_ge(s_obe.h, og - 1)
                                src_flat = otok.ap()[:, tq, :, :].rearrange(
                                    "p h d -> p (h d)")
                                ins = None
                                for j in range(4):
                                    cc = g * 4 + j
                                    ins = t.matmul(
                                        tailp.ap()[:, (og % 2) * 512 + j * 128:
                                                   (og % 2) * 512 + (j + 1) * 128],
                                        src_flat[:, cc * 128:(cc + 1) * 128],
                                        eye_sb.ap(), start=True, stop=True)
                                s_obt.inc(ins)

                        @blk.vector
                        def _(vec):
                            for g in range(2):
                                og = og0 + g
                                vec.wait_ge(s_obt.h, og + 1)
                                dst = otok.ap()[:, tq, :, :].rearrange(
                                    "p h d -> p (h d)").rearrange(
                                    "p (c f) -> p c f", f=128)[
                                    :, g * 4:(g + 1) * 4, :]
                                src = tailp.ap()[:, (og % 2) * 512:
                                                 (og % 2 + 1) * 512].rearrange(
                                    "p (c f) -> p c f", f=128)
                                ins = vec.tensor_copy(dst, src)
                                s_obe.inc(ins)

                        @blk.tensor
                        def _(t):
                            t.wait_ge(s_obe.h, og0 + 2)
                            if tq == 0:
                                t.wait_ge(s_oev.h, 2 * (qs * NPAIR + NPAIR))
                                t.wait_ge(s_w2.h, s_w2.n)
                            else:
                                t.wait_ge(s_r1.h, R)
                            src_flat = otok.ap()[:, tq, :, :].rearrange(
                                "p h d -> p (h d)")
                            ins = None
                            for cch in range(8):
                                for ds in range(2):
                                    ins = t.matmul(
                                        oX[ds].ap()[:, :],
                                        src_flat[:, cch * 128:(cch + 1) * 128],
                                        wout_sb.ap()[:, cch,
                                                     ds * 512:(ds + 1) * 512],
                                        start=(cch == 0), stop=(cch == 7))
                            s_op.inc(ins)

                        @blk.vector
                        def _(vec):
                            vec.wait_ge(s_op.h, R + 1)
                            vec.wait_ge(s_xrp[R % 2].h, 16 * (R // 2 + 1))
                            if R >= 2:
                                vec.wait_ge(s_r1wp[R % 2].h, 16 * (R // 2))
                            for ds in range(2):
                                vec.tensor_tensor(
                                    r1b.ap()[:, par, ds * 512:(ds + 1) * 512],
                                    oX[ds].ap(),
                                    bout_bc.ap()[:, ds * 512:(ds + 1) * 512],
                                    ALU.add)
                            ins = vec.tensor_tensor(r1b.ap()[:, par, :],
                                                    r1b.ap()[:, par, :],
                                                    xresid.ap()[:, par, :],
                                                    ALU.add)
                            s_r1.inc(ins)

                        @blk.sync
                        def _(sy):
                            sy.wait_ge(s_r1.h, R + 1)
                            ins = sy.dma_start(
                                out=r1_d[R * 128:(R + 1) * 128, :],
                                in_=r1b.ap()[:, par, :])
                            s_r1wp[R % 2].inc(ins, 16)

                        emit_wq_unit()

                @blk.vector
                def _(vec):
                    p2bar.inc(vec.engine_nop())

                @blk.scalar
                def _(sc):
                    p2bar.inc(sc.nop())

                @blk.sync
                def _(sy):
                    sy.wait_ge(s_r1wp[0].h, s_r1wp[0].n)
                    sy.wait_ge(s_r1wp[1].h, s_r1wp[1].n)
                    p2bar.inc(sy.nop())

                while wq_next[0] < N_WQ:
                    emit_wq_unit()

        with ExitStack() as p3s:
            e3 = p3s.enter_context
            wgT = e3(nc.sbuf_tensor("wgT", [128, 8, IFF], BF16))
            wuT = e3(nc.sbuf_tensor("wuT", [128, 8, IFF], BF16))
            wdT = e3(nc.sbuf_tensor("wdT", [128, 22, D], BF16))
            wqin = e3(nc.sbuf_tensor("wqin", [128, 1, 1024], BF16))
            wdin = e3(nc.sbuf_tensor("wdin", [128, 1, 4, 128], BF16))
            rln = e3(nc.sbuf_tensor("rln", [128, 2, D], F32))
            y1b = e3(nc.sbuf_tensor("y1b", [128, 2, D], F32))
            xqs = e3(nc.sbuf_tensor("xqs", [128, 1, D], BF16))
            xqt = e3(nc.sbuf_tensor("xqt", [128, 2, 8, 128], BF16))
            sgb = e3(nc.sbuf_tensor("sgb", [128, 1, 512], F32))
            hbuf = e3(nc.sbuf_tensor("hbuf", [128, 512], F32))
            hqb = e3(nc.sbuf_tensor("hqb", [128, 1, 512], BF16))
            hqt = e3(nc.sbuf_tensor("hqt", [128, 1, 4, 128], BF16))
            g_ps = e3(nc.psum_tensor("g_ps", [128, 1024], F32))
            u_ps = e3(nc.psum_tensor("u_ps", [128, 1024], F32))
            d_ps = e3(nc.psum_tensor("d_ps", [128, 1024], F32))
            trp = e3(nc.psum_tensor("trp", [128, 1024], F32))

            grp3 = [0]
            tre_of_grp = {}

            def pe_tgroup3(t, srcs):
                g = grp3[0]
                grp3[0] = g + 1
                half = g % 2
                if g >= 2:
                    t.wait_ge(s_tre.h, g - 1)
                ins = None
                for j, ap_ in enumerate(srcs):
                    ins = t.matmul(
                        trp.ap()[:, half * 512 + j * 128:
                                 half * 512 + j * 128 + 128],
                        ap_, eye_sb.ap(), start=True, stop=True)
                s_trp.inc(ins)
                return g

            def vec_egroup3(vec, g, dst_ap, nsub=4):
                vec.wait_ge(s_trp.h, g + 1)
                half = g % 2
                src = trp.ap()[:, half * 512:half * 512 + nsub * 128].rearrange(
                    "p (c f) -> p c f", f=128)
                ins = vec.tensor_copy(dst_ap, src)
                s_tre.inc(ins)

            def sc_egroup3(sc, g, dst_ap, nsub=4):
                sc.wait_ge(s_trp.h, g + 1)
                half = g % 2
                src = trp.ap()[:, half * 512:half * 512 + nsub * 128].rearrange(
                    "p (c f) -> p c f", f=128)
                ins = sc.activation(dst_ap, src, ACT.Copy)
                s_tre.inc(ins)

            firstp3 = {"t": True, "sy": True, "v": True, "sc": True}

            load_i = [0]
            wq_ready = {}

            def p3_load_unit(name, ti, co, w, dst_slot):
                li = load_i[0]
                load_i[0] = li + 1
                u = WQ_UNIT_IDX[(name, ti, co)]

                @blk.sync
                def _(sy):
                    if firstp3["sy"]:
                        sy.wait_ge(p2bar.h, 4)
                        sy.wait_ge(s_wfwp[0].h, s_wfwp[0].n)
                        sy.wait_ge(s_wfwp[1].h, s_wfwp[1].n)
                        firstp3["sy"] = False
                    if li >= 1:
                        sy.wait_ge(s_trp.h, wq_ready.get(("ld", li - 1), 0))
                    ins = sy.dma_start(
                        out=dst_slot,
                        in_=wq_out_d[name][ti * 128:(ti + 1) * 128, co:co + w])
                    s_wqi.inc(ins, 16)
                return li

            wgwu = {"wg": (wgT, wgq_d), "wu": (wuT, wuq_d)}
            order = []
            for sl in range(6):
                lo = sl * 4
                hi = min(lo + 4, 22)
                for nm in ("wg", "wu"):
                    for ti in range(lo, hi):
                        order.append((nm, ti))
            for nm, ti in order:
                par = 0
                li = p3_load_unit(nm, ti, 0, 1024, wqin.ap()[:, 0, :])
                gidx = []

                @blk.tensor
                def _(t):
                    if firstp3["t"]:
                        t.wait_ge(p2bar.h, 4)
                        firstp3["t"] = False
                    t.wait_ge(s_wqi.h, 16 * (li + 1))
                    for g in range(2):
                        srcs = [wqin.ap()[:, par,
                                          (g * 4 + j) * 128:(g * 4 + j + 1) * 128]
                                for j in range(4)]
                        gidx.append(pe_tgroup3(t, srcs))

                @blk.vector
                def _(vec):
                    if firstp3["v"]:
                        vec.wait_ge(p2bar.h, 4)
                        firstp3["v"] = False
                    dstT = wgT if nm == "wg" else wuT
                    for g in range(2):
                        vec_egroup3(vec, gidx[g],
                                    dstT.ap()[:, g * 4:(g + 1) * 4,
                                              ti * 128:(ti + 1) * 128])
                wq_ready[("ld", li)] = s_trp.n
                wq_ready[(nm, ti)] = s_tre.n

            for ti in range(8):
                for gc in range(6):
                    nch = 4 if gc < 5 else 2
                    par = 0
                    li = load_i[0]
                    load_i[0] += 1
                    u = WQ_UNIT_IDX[("wd", ti, (gc * 4 * 128 // 1024) * 1024)] \
                        if False else None

                    @blk.sync
                    def _(sy):
                        if li >= 1:
                            sy.wait_ge(s_trp.h, wq_ready.get(("ld", li - 1), 0))
                        for j in range(nch):
                            cc = gc * 4 + j
                            ins = sy.dma_start(
                                out=wdin.ap()[:, par, j, :],
                                in_=wdq_d[ti * 128:(ti + 1) * 128,
                                          cc * 128:(cc + 1) * 128])
                            s_wqi.inc(ins, 16)

                    gidx = []

                    @blk.tensor
                    def _(t):
                        t.wait_ge(s_wqi.h, s_wqi.n)
                        srcs = [wdin.ap()[:, par, j, :] for j in range(nch)]
                        gidx.append(pe_tgroup3(t, srcs))

                    @blk.vector
                    def _(vec):
                        vec_egroup3(vec, gidx[0],
                                    wdT.ap()[:, gc * 4:gc * 4 + nch,
                                             ti * 128:(ti + 1) * 128],
                                    nsub=nch)
                    wq_ready[("ld", li)] = s_trp.n
            WD_DONE = s_tre.n

            xqt_done = {}
            hq_trp = {}
            xqt_trp = {}

            def emit_rd(j):
                @blk.sync
                def _(sy):
                    if j == 0:
                        sy.wait_ge(s_r1wp[0].h, s_r1wp[0].n)
                        sy.wait_ge(s_r1wp[1].h, s_r1wp[1].n)
                    if j >= 2:
                        sy.wait_ge(s_xq3.h, j - 1)
                        sy.wait_ge(s_outp[j % 2].h, 16 * (j // 2))
                    ins = sy.dma_start(out=rln.ap()[:, j % 2, :],
                                       in_=r1_d[j * 128:(j + 1) * 128, :])
                    s_r1ip[j % 2].inc(ins, 16)

            def emit_ln1_stats(j):
                @blk.vector
                def _(vec):
                    vec.wait_ge(s_r1ip[j % 2].h, 16 * (j // 2 + 1))
                    ln_stats(vec, rln.ap()[:, j % 2, :])

            def emit_ln1_sqrt(j):
                @blk.scalar
                def _(sc):
                    if firstp3["sc"]:
                        sc.wait_ge(p2bar.h, 4)
                        firstp3["sc"] = False
                    ln_sqrt(sc, vself.n)

            def emit_ln1_finqx(j):
                @blk.vector
                def _(vec):
                    ln_fin(vec, rln.ap()[:, j % 2, :], y1b.ap()[:, j % 2, :],
                           sself.n)
                    if j >= 1:
                        vec.wait_ge(s_trp.h, xqt_trp.get(j - 1, 0))
                    ins = quant_slab(vec, y1b.ap()[:, j % 2, :],
                                     xqs.ap()[:, 0, :], D)
                    s_xq3.inc(ins)

            def emit_xqT(j):
                gidx = []

                @blk.tensor
                def _(t):
                    t.wait_ge(s_xq3.h, j + 1)
                    for g in range(2):
                        srcs = [xqs.ap()[:, 0,
                                         (g * 4 + k) * 128:
                                         (g * 4 + k + 1) * 128]
                                for k in range(4)]
                        gidx.append(pe_tgroup3(t, srcs))
                xqt_trp[j] = s_trp.n
                return gidx

            def emit_xqT_evac(j, gidx):
                @blk.scalar
                def _(sc):
                    for g in range(2):
                        sc_egroup3(sc, gidx[g],
                                   xqt.ap()[:, j % 2, g * 4:(g + 1) * 4, :])
                xqt_done[j] = s_tre.n

            def emit_gu(t_i, sl):
                W = 512 if sl < 5 else 256
                SL = 6 * t_i + sl
                p2_ = sl % 2
                par = t_i % 2

                @blk.tensor
                def _(t):
                    if sl == 0:
                        t.wait_ge(s_tre.h, xqt_done[t_i])
                    if t_i == 0:
                        t.wait_ge(s_tre.h,
                                  max(wq_ready[("wg", min(sl * 4 + 3, 21))],
                                      wq_ready[("wu", min(sl * 4 + 3, 21))]))
                    if SL >= 2:
                        t.wait_ge(s_sil.h, SL - 1)
                    ins = None
                    for cch in range(8):
                        ins = t.matmul(
                            g_ps.ap()[:, p2_ * 512:p2_ * 512 + W],
                            xqt.ap()[:, par, cch, :],
                            wgT.ap()[:, cch, sl * 512:sl * 512 + W],
                            start=(cch == 0), stop=(cch == 7))
                    s_gu.inc(ins)
                    if SL >= 2:
                        t.wait_ge(s_h.h, SL - 1)
                    for cch in range(8):
                        ins = t.matmul(
                            u_ps.ap()[:, p2_ * 512:p2_ * 512 + W],
                            xqt.ap()[:, par, cch, :],
                            wuT.ap()[:, cch, sl * 512:sl * 512 + W],
                            start=(cch == 0), stop=(cch == 7))
                    s_gu.inc(ins)

            def emit_silu(t_i, sl):
                W = 512 if sl < 5 else 256
                SL = 6 * t_i + sl
                p2_ = sl % 2

                @blk.scalar
                def _(sc):
                    sc.wait_ge(s_gu.h, 2 * SL + 1)
                    if SL >= 1:
                        sc.wait_ge(s_h.h, SL)
                    ins = sc.activation(sgb.ap()[:, 0, 0:W],
                                        g_ps.ap()[:, p2_ * 512:p2_ * 512 + W],
                                        ACT.Silu)
                    s_sil.inc(ins)

            def emit_hq(t_i, sl):
                W = 512 if sl < 5 else 256
                SL = 6 * t_i + sl
                p2_ = sl % 2

                @blk.vector
                def _(vec):
                    vec.wait_ge(s_gu.h, 2 * SL + 2)
                    vec.wait_ge(s_sil.h, SL + 1)
                    ins = vec.tensor_tensor(
                        hbuf.ap()[:, 0:W], sgb.ap()[:, 0, 0:W],
                        u_ps.ap()[:, p2_ * 512:p2_ * 512 + W], ALU.mult)
                    s_h.inc(ins)
                    if SL >= 1:
                        vec.wait_ge(s_trp.h, hq_trp.get(SL - 1, 0))
                    ins = quant_slab(vec, hbuf.ap()[:, 0:W],
                                     hqb.ap()[:, 0, 0:W], W)
                    s_hq.inc(ins)

            def emit_hqT(t_i, sl):
                W = 512 if sl < 5 else 256
                SL = 6 * t_i + sl
                nch = W // 128
                gidx2 = []

                @blk.tensor
                def _(t):
                    t.wait_ge(s_hq.h, SL + 1)
                    srcs = [hqb.ap()[:, 0, k * 128:(k + 1) * 128]
                            for k in range(nch)]
                    gidx2.append(pe_tgroup3(t, srcs))
                hq_trp[SL] = s_trp.n
                return gidx2

            def emit_hq_evac(t_i, sl, gidx2):
                W = 512 if sl < 5 else 256
                nch = W // 128

                @blk.scalar
                def _(sc):
                    sc_egroup3(sc, gidx2[0], hqt.ap()[:, 0, 0:nch, :],
                               nsub=nch)
                return s_tre.n

            def emit_down(t_i, sl, hq_evac):
                W = 512 if sl < 5 else 256
                nch = W // 128

                @blk.tensor
                def _(t):
                    t.wait_ge(s_tre.h, hq_evac)
                    if t_i == 0 and sl == 0:
                        t.wait_ge(s_tre.h, WD_DONE)
                    if sl == 0 and t_i >= 1:
                        t.wait_ge(s_y2e.h, t_i)
                    ins = None
                    for c4 in range(nch):
                        ic = sl * 4 + c4
                        for ds in range(2):
                            ins = t.matmul(
                                d_ps.ap()[:, ds * 512:(ds + 1) * 512],
                                hqt.ap()[:, 0, c4, :],
                                wdT.ap()[:, ic, ds * 512:(ds + 1) * 512],
                                start=(ic == 0), stop=(ic == 21))
                    if sl == 5:
                        s_dn.inc(ins)

            def emit_tail(t_i):
                par = t_i % 2

                @blk.vector
                def _(vec):
                    vec.wait_ge(s_dn.h, t_i + 1)
                    ins = vec.tensor_tensor(rln.ap()[:, par, :], d_ps.ap(),
                                            y1b.ap()[:, par, :], ALU.add)
                    s_y2e.inc(ins)
                    ln_stats(vec, rln.ap()[:, par, :])

                @blk.scalar
                def _(sc):
                    ln_sqrt(sc, vself.n)

                @blk.vector
                def _(vec):
                    ins = ln_fin(vec, rln.ap()[:, par, :],
                                 rln.ap()[:, par, :], sself.n)
                    s_ln2.inc(ins)

                @blk.sync
                def _(sy):
                    sy.wait_ge(s_ln2.h, t_i + 1)
                    ins = sy.dma_start(out=y[t_i * 128:(t_i + 1) * 128, :],
                                       in_=rln.ap()[:, par, :])
                    s_outp[t_i % 2].inc(ins, 16)

            emit_rd(0)
            emit_ln1_stats(0)
            emit_ln1_sqrt(0)
            emit_ln1_finqx(0)
            emit_xqT_evac(0, emit_xqT(0))

            for t_i in range(NQT):
                nxt = t_i + 1 if t_i + 1 < NQT else None
                if nxt is not None:
                    emit_rd(nxt)
                emit_gu(t_i, 0)
                emit_silu(t_i, 0)
                emit_hq(t_i, 0)
                emit_gu(t_i, 1)
                emit_silu(t_i, 1)
                ev0 = emit_hq_evac(t_i, 0, emit_hqT(t_i, 0))
                emit_down(t_i, 0, ev0)
                if nxt is not None:
                    emit_ln1_stats(nxt)
                    emit_ln1_sqrt(nxt)
                emit_hq(t_i, 1)
                emit_gu(t_i, 2)
                emit_silu(t_i, 2)
                ev1 = emit_hq_evac(t_i, 1, emit_hqT(t_i, 1))
                emit_down(t_i, 1, ev1)
                if nxt is not None:
                    emit_ln1_finqx(nxt)
                emit_hq(t_i, 2)
                emit_gu(t_i, 3)
                emit_silu(t_i, 3)
                ev2 = emit_hq_evac(t_i, 2, emit_hqT(t_i, 2))
                emit_down(t_i, 2, ev2)
                if nxt is not None:
                    emit_xqT_evac(nxt, emit_xqT(nxt))
                emit_hq(t_i, 3)
                emit_gu(t_i, 4)
                emit_silu(t_i, 4)
                ev3 = emit_hq_evac(t_i, 3, emit_hqT(t_i, 3))
                emit_down(t_i, 3, ev3)
                emit_hq(t_i, 4)
                emit_gu(t_i, 5)
                emit_silu(t_i, 5)
                ev4 = emit_hq_evac(t_i, 4, emit_hqT(t_i, 4))
                emit_down(t_i, 4, ev4)
                emit_hq(t_i, 5)
                ev5 = emit_hq_evac(t_i, 5, emit_hqT(t_i, 5))
                emit_down(t_i, 5, ev5)
                emit_tail(t_i)

            @blk.sync
            def _(sy):
                sy.wait_ge(s_outp[0].h, s_outp[0].n)
                sy.wait_ge(s_outp[1].h, s_outp[1].n)
                if DEBUG[0]:
                    for nm3, t3 in (("wgT", wgT), ("wdT", wdT), ("xqt", xqt),
                                    ("y1b", y1b), ("hqt", hqt), ("rlnv", rln)):
                        sy.dma_start(out=dbg[nm3], in_=t3.ap()).then_inc(
                            s_xr.h, 16)
                        s_xr.n += 16
                    sy.dma_start(out=dbg["st0"], in_=dbgsb.ap()).then_inc(
                        s_xr.h, 16)
                    s_xr.n += 16
                    sy.dma_start(out=dbg["p2"], in_=dbgp2.ap()).then_inc(
                        s_xr.h, 16)
                    s_xr.n += 16
                    for name, srcd in (("ktr", ktr_d), ("qtr", qtr_d),
                                       ("r1", r1_d), ("wgq", wgq_d)):
                        sy.dma_start(out=dbg[name], in_=srcd).then_inc(
                            s_xr.h, 16)
                        s_xr.n += 16
                    sy.wait_ge(s_xr.h, s_xr.n)

    return nc


_NC_CACHE = [None]


def kernel(L0_attn_x, L0_attn_w_q, L0_attn_w_kv, L0_attn_w_out, L0_attn_b_out,
           L0_attn_scale, L0_ffn_w_gate, L0_ffn_w_up, L0_ffn_w_down):
    xin = np.ascontiguousarray(np.asarray(L0_attn_x, dtype=np.float32))
    w_q = np.ascontiguousarray(np.asarray(L0_attn_w_q, dtype=np.float32))
    w_kv = np.ascontiguousarray(np.asarray(L0_attn_w_kv, dtype=np.float32))
    w_out = np.ascontiguousarray(np.asarray(L0_attn_w_out, dtype=np.float32))
    b_out = np.asarray(L0_attn_b_out, dtype=np.float32).reshape(1, D)
    scal = np.asarray(L0_attn_scale, dtype=np.float32).reshape(1, H)
    wg = np.ascontiguousarray(np.asarray(L0_ffn_w_gate, dtype=np.float32))
    wu = np.ascontiguousarray(np.asarray(L0_ffn_w_up, dtype=np.float32))
    wd = np.ascontiguousarray(np.asarray(L0_ffn_w_down, dtype=np.float32))
    eyev = np.eye(128, dtype=np.float32)

    if _NC_CACHE[0] is None:
        _NC_CACHE[0] = build_core()
    nc = _NC_CACHE[0]

    in_maps = []
    for c in range(8):
        b, qh = c // 2, c % 2
        xc = np.ascontiguousarray(np.roll(xin[b], -qh * QTOK, axis=0))
        in_maps.append({
            "x": xc, "w_q": w_q, "w_kv": w_kv, "w_out": w_out,
            "b_out": b_out, "scal": scal, "wg": wg, "wu": wu, "wd": wd,
            "eye": eyev,
        })

    res = run_bass_kernel_spmd(nc, in_maps, core_ids=list(range(8)))
    out = np.empty((4, SEQ, D), dtype=np.float32)
    for c in range(8):
        b, qh = c // 2, c % 2
        out[b, qh * QTOK:(qh + 1) * QTOK] = res.results[c]["y"]
    return out


# version 17 (final)
# speedup vs baseline: 1.1182x; 1.0116x over previous
import numpy as np
import concourse.bass as bass
import concourse.mybir as mybir
from concourse.bass_utils import run_bass_kernel_spmd
from contextlib import ExitStack

F32 = mybir.dt.float32
BF16 = mybir.dt.bfloat16
ALU = mybir.AluOpType
ACT = mybir.ActivationFunctionType
AX = mybir.AxisListType.X

D = 1024
H = 16
DH = 64
NPAIR = 8
SEQ = 4096
QTOK = 2048
NKT = SEQ // 128
NQT = QTOK // 128
QS = 1024
NQS = QTOK // QS
NTQ = QS // 128
IFF = 2816
GB = 16
MID2 = [0.0625, 0.5625, 1.5625, 3.0625, 6.25, 12.25, 25.0]
WQ_UNITS = (
    [("wg", i, 0, 1024) for i in range(22)]
    + [("wu", i, 0, 1024) for i in range(22)]
    + [("wd", i, s * 1024, 1024 if s < 2 else 768) for i in range(8) for s in range(3)]
)
N_WQ = len(WQ_UNITS)
DEBUG = [False]
WQ_UNIT_IDX = {(nm, ti, co): i for i, (nm, ti, co, _w) in enumerate(WQ_UNITS)}


class Sem:
    REGISTRY = None

    def __init__(self, stack, nc, name):
        self.h = stack.enter_context(nc.semaphore(name))
        self.n = 0
        if Sem.REGISTRY is not None:
            Sem.REGISTRY.append(self.h)

    def inc(self, instr, k=1):
        self.n += k
        instr.then_inc(self.h, k)
        return self.n


def build_core():
    nc = bass.Bass("TRN2", target_bir_lowering=False,
                   detect_race_conditions=False)

    x = nc.dram_tensor("x", [SEQ, D], F32, kind="ExternalInput").ap()
    w_q = nc.dram_tensor("w_q", [D, D], F32, kind="ExternalInput").ap()
    w_kv = nc.dram_tensor("w_kv", [D, 2 * D], F32, kind="ExternalInput").ap()
    w_out = nc.dram_tensor("w_out", [D, D], F32, kind="ExternalInput").ap()
    b_out = nc.dram_tensor("b_out", [1, D], F32, kind="ExternalInput").ap()
    scal = nc.dram_tensor("scal", [1, H], F32, kind="ExternalInput").ap()
    wgi = nc.dram_tensor("wg", [IFF, D], F32, kind="ExternalInput").ap()
    wui = nc.dram_tensor("wu", [IFF, D], F32, kind="ExternalInput").ap()
    wdi = nc.dram_tensor("wd", [D, IFF], F32, kind="ExternalInput").ap()
    eye = nc.dram_tensor("eye", [128, 128], F32, kind="ExternalInput").ap()
    y = nc.dram_tensor("y", [QTOK, D], F32, kind="ExternalOutput").ap()

    ktr_d = nc.dram_tensor("ktr_d", [NPAIR, 128, SEQ], BF16, kind="Internal").ap()
    qtr_d = nc.dram_tensor("qtr_d", [NPAIR, 128, QTOK], BF16, kind="Internal").ap()
    r1_d = nc.dram_tensor("r1_d", [QTOK, D], F32, kind="Internal").ap()
    wgq_d = nc.dram_tensor("wgq_d", [IFF, D], BF16, kind="Internal").ap()
    wuq_d = nc.dram_tensor("wuq_d", [IFF, D], BF16, kind="Internal").ap()
    wdq_d = nc.dram_tensor("wdq_d", [D, IFF], BF16, kind="Internal").ap()
    wq_in_d = {"wg": wgi, "wu": wui, "wd": wdi}
    wq_out_d = {"wg": wgq_d, "wu": wuq_d, "wd": wdq_d}
    dbg = {}
    if DEBUG[0]:
        dbg["ktr"] = nc.dram_tensor("dbg_ktr", [NPAIR, 128, SEQ], BF16,
                                    kind="ExternalOutput").ap()
        dbg["qtr"] = nc.dram_tensor("dbg_qtr", [NPAIR, 128, QTOK], BF16,
                                    kind="ExternalOutput").ap()
        dbg["r1"] = nc.dram_tensor("dbg_r1", [QTOK, D], F32,
                                   kind="ExternalOutput").ap()
        dbg["wgq"] = nc.dram_tensor("dbg_wgq", [IFF, D], BF16,
                                    kind="ExternalOutput").ap()
        dbg["v"] = nc.dram_tensor("dbg_v", [128, NKT, H, 65], BF16,
                                  kind="ExternalOutput").ap()
        dbg["kst"] = nc.dram_tensor("dbg_kst", [128, 4, H], F32,
                                    kind="ExternalOutput").ap()
        dbg["khat"] = nc.dram_tensor("dbg_khat", [128, 2, D], BF16,
                                     kind="ExternalOutput").ap()
        dbg["ktr01"] = nc.dram_tensor("dbg_ktr01", [2, 128, 8, 128], BF16,
                                      kind="ExternalOutput").ap()
        dbg["st0"] = nc.dram_tensor("dbg_st0", [128, 8, 64], F32,
                                    kind="ExternalOutput").ap()
        dbg["p2"] = nc.dram_tensor("dbg_p2", [128, 6, 512], BF16,
                                   kind="ExternalOutput").ap()
        dbg["wgT"] = nc.dram_tensor("dbg_wgT", [128, 8, IFF], BF16,
                                    kind="ExternalOutput").ap()
        dbg["wdT"] = nc.dram_tensor("dbg_wdT", [128, 22, D], BF16,
                                    kind="ExternalOutput").ap()
        dbg["xqt"] = nc.dram_tensor("dbg_xqt", [128, 2, 8, 128], BF16,
                                    kind="ExternalOutput").ap()
        dbg["y1b"] = nc.dram_tensor("dbg_y1b", [128, 2, D], F32,
                                    kind="ExternalOutput").ap()
        dbg["hqt"] = nc.dram_tensor("dbg_hqt", [128, 1, 4, 128], BF16,
                                    kind="ExternalOutput").ap()
        dbg["rlnv"] = nc.dram_tensor("dbg_rlnv", [128, 2, D], F32,
                                     kind="ExternalOutput").ap()

    st = ExitStack()
    with st:
        e = st.enter_context
        Sem.REGISTRY = []

        def S(name):
            return Sem(st, nc, name)

        eye_sb = e(nc.sbuf_tensor("eye_sb", [128, 128], BF16))
        ebc = e(nc.sbuf_tensor("ebc", [128, H], F32))
        negm = e(nc.sbuf_tensor("negm", [128, H], F32))
        negmsh = e(nc.sbuf_tensor("negmsh", [128, 8], F32))
        bout_bc = e(nc.sbuf_tensor("bout_bc", [128, D], F32))
        qbA = e(nc.sbuf_tensor("qbA", [128, 1024], F32))
        qh1 = e(nc.sbuf_tensor("qh1", [128, 1024], BF16))
        qh2 = e(nc.sbuf_tensor("qh2", [128, 1024], BF16))
        qh3 = e(nc.sbuf_tensor("qh3", [128, 1024], BF16))
        qh4 = e(nc.sbuf_tensor("qh4", [128, 1024], BF16))
        qg1 = e(nc.sbuf_tensor("qg1", [128, 64], F32))
        qg2 = e(nc.sbuf_tensor("qg2", [128, 64], F32))
        wstg = e(nc.sbuf_tensor("wstg", [128, 2, 1024], F32))
        wqo = e(nc.sbuf_tensor("wqo", [128, 2, 1024], BF16))
        lnm = e(nc.sbuf_tensor("lnm", [128, 8], F32))
        bst = e(nc.sbuf_tensor("bst", [128, 2, 6], F32))
        bagg = e(nc.sbuf_tensor("bagg", [128, 2], F32))
        rtw = e(nc.sbuf_tensor("rtw", [128, 8], F32))
        rsw = e(nc.sbuf_tensor("rsw", [128, 8], F32))
        dbgsb = e(nc.sbuf_tensor("dbgsb", [128, 8, 64], F32))
        dbgp2 = e(nc.sbuf_tensor("dbgp2", [128, 6, 512], BF16))
        lnx = e(nc.sbuf_tensor("lnx", [128, 1024], F32))

        vself = S("vself")
        sself = S("sself")
        s_misc = S("s_misc")
        s_set = S("s_set")
        p1bar = S("p1bar")
        p2bar = S("p2bar")
        s_x = S("s_x")
        s_xp = [S("s_xp0"), S("s_xp1")]
        s_w = S("s_w")
        s_w2 = S("s_w2")
        s_xt = S("s_xt")
        s_xte = S("s_xte")
        s_kv = S("s_kv")
        s_qm = S("s_qm")
        s_sq = S("s_sq")
        s_vst = S("s_vst")
        s_ssq = S("s_ssq")
        s_nrm = S("s_nrm")
        s_ktw = S("s_ktw")
        s_qtw = S("s_qtw")
        s_ktwp = [S("s_ktwp0"), S("s_ktwp1")]
        s_qtwp = [S("s_qtwp0"), S("s_qtwp1")]
        s_wfi = S("s_wfi")
        s_wfq = S("s_wfq")
        s_wfw = S("s_wfw")
        s_wfip = [S("s_wfip0"), S("s_wfip1")]
        s_wfwp = [S("s_wfwp0"), S("s_wfwp1")]
        s_kin = S("s_kin")
        s_qin = S("s_qin")
        s_kinp = [S("s_kinp0"), S("s_kinp1")]
        s_qinp = [S("s_qinp0"), S("s_qinp1")]
        s_simA = S("s_simA")
        s_simB = S("s_simB")
        s_expA = S("s_expA")
        s_expB = S("s_expB")
        s_avA = S("s_avA")
        s_avB = S("s_avB")
        s_oev = S("s_oev")
        s_otr = S("s_otr")
        s_ofin = S("s_ofin")
        s_obt = S("s_obt")
        s_obe = S("s_obe")
        s_op = S("s_op")
        s_xr = S("s_xr")
        s_xrp = [S("s_xrp0"), S("s_xrp1")]
        s_r1 = S("s_r1")
        s_r1w = S("s_r1w")
        s_r1wp = [S("s_r1wp0"), S("s_r1wp1")]
        s_r1i = S("s_r1i")
        s_r1ip = [S("s_r1ip0"), S("s_r1ip1")]
        s_wqi = S("s_wqi")
        s_trp = S("s_trp")
        s_tre = S("s_tre")
        s_xq3 = S("s_xq3")
        s_gu = S("s_gu")
        s_sil = S("s_sil")
        s_h = S("s_h")
        s_hq = S("s_hq")
        s_dn = S("s_dn")
        s_y2e = S("s_y2e")
        s_ln2 = S("s_ln2")
        s_out = S("s_out")
        s_outp = [S("s_outp0"), S("s_outp1")]

        with nc.Block() as blk0:
            @blk0.sync
            def _(sy):
                for hh_ in Sem.REGISTRY:
                    sy.sem_clear(hh_)

        blk = e(nc.Block())

        def vedge(vec):
            c = vself.inc(vec.engine_nop())
            vec.wait_ge(vself.h, c)

        def vrecip(vec, out, in_):
            vedge(vec)
            vec.reciprocal(out, in_)
            vedge(vec)

        def quant_slab(vec, src_ap, dst_ap, width):
            G = width // GB
            a = qbA.ap()[:, 0:width]
            sq = qh1.ap()[:, 0:width]
            b = qh2.ap()[:, 0:width]
            c = qh3.ap()[:, 0:width]
            d = qh4.ap()[:, 0:width]
            g1 = qg1.ap()[:, 0:G]
            g2 = qg2.ap()[:, 0:G]
            srcg = src_ap.rearrange("p (g s) -> p g s", s=GB)
            vec.tensor_reduce(g1, srcg, AX, ALU.max, apply_absolute_value=True)
            vedge(vec)
            vec.tensor_scalar(g1, g1, 1.0 / 6.0, 1e-12, ALU.mult, ALU.max)
            vrecip(vec, g2, g1)
            vec.tensor_tensor(g2, g2, g2, ALU.mult)
            vedge(vec)
            g2b = g2.unsqueeze(2).broadcast_to([128, G, GB])
            g1b = g1.unsqueeze(2).broadcast_to([128, G, GB])
            ag = a.rearrange("p (g s) -> p g s", s=GB)
            sqg = sq.rearrange("p (g s) -> p g s", s=GB)
            vec.tensor_tensor(a, src_ap, src_ap, ALU.mult)
            vec.tensor_tensor(ag, ag, g2b, ALU.mult)
            vec.tensor_scalar(b, a, MID2[0], None, ALU.is_gt)
            for m2 in MID2[1:]:
                vec.scalar_tensor_tensor(b, a, m2, b, ALU.is_gt, ALU.add)
            vec.tensor_scalar(c, b, 4.0, 0.0, ALU.subtract, ALU.max)
            vec.tensor_scalar(d, b, 6.0, 0.0, ALU.subtract, ALU.max)
            vec.scalar_tensor_tensor(c, d, 2.0, c, ALU.mult, ALU.add)
            vec.tensor_tensor(c, c, b, ALU.add)
            vec.tensor_scalar(a, src_ap, 0.0, None, ALU.is_ge)
            vec.scalar_tensor_tensor(ag, ag, 0.5, g1b, ALU.subtract, ALU.mult)
            return vec.tensor_tensor(dst_ap, c, a, ALU.mult)

        wq_next = [0]

        def emit_wq_unit():
            u = wq_next[0]
            if u >= N_WQ:
                return
            wq_next[0] = u + 1
            name, ti, co, w = WQ_UNITS[u]
            par = u % 2

            @blk.sync
            def _(sy):
                if u >= 2:
                    sy.wait_ge(s_wfq.h, u - 1)
                ins = sy.dma_start(
                    out=wstg.ap()[:, par, 0:w],
                    in_=wq_in_d[name][ti * 128:(ti + 1) * 128, co:co + w])
                s_wfip[u % 2].inc(ins, 16)

            @blk.vector
            def _(vec):
                vec.wait_ge(s_wfip[u % 2].h, 16 * (u // 2 + 1))
                if u >= 2:
                    vec.wait_ge(s_wfwp[u % 2].h, 16 * (u // 2))
                ins = quant_slab(vec, wstg.ap()[:, par, 0:w],
                                 wqo.ap()[:, par, 0:w], w)
                s_wfq.inc(ins)

            @blk.sync
            def _(sy):
                sy.wait_ge(s_wfq.h, u + 1)
                ins = sy.dma_start(
                    out=wq_out_d[name][ti * 128:(ti + 1) * 128, co:co + w],
                    in_=wqo.ap()[:, par, 0:w])
                s_wfwp[u % 2].inc(ins, 16)

        def ln_stats(vec, src_ap):
            vec.bn_stats(bst.ap()[:, 0, :], src_ap[:, 0:512])
            vec.bn_stats(bst.ap()[:, 1, :], src_ap[:, 512:1024])
            vedge(vec)
            vec.bn_aggr(bagg.ap(), bst.ap().rearrange("p a b -> p (a b)"))
            vedge(vec)
            ins = vec.tensor_scalar(
                rtw.ap(), bagg.ap()[:, 1:2].broadcast_to([128, 8]),
                1.0, 1e-5, ALU.mult, ALU.add)
            return vself.inc(ins)

        def ln_sqrt(sc, wait_c):
            sc.wait_ge(vself.h, wait_c)
            sc.activation(rsw.ap(), rtw.ap(), ACT.Ln)
            return sself.inc(sc.activation(rsw.ap(), rsw.ap(), ACT.Exp,
                                           scale=-0.5))

        def ln_fin(vec, src_ap, dst_ap, wait_s):
            vec.wait_ge(sself.h, wait_s)
            vec.tensor_scalar(lnx.ap(), src_ap, bagg.ap()[:, 0:1],
                              None, ALU.subtract)
            return vec.tensor_scalar(dst_ap, lnx.ap(), rsw.ap()[:, 0:1],
                                     None, ALU.mult)

        @blk.gpsimd
        def _(gp):
            s_misc.inc(gp.dma_start(out=eye_sb.ap(), in_=eye), 16)
            s_misc.inc(gp.dma_start(
                out=ebc.ap(), in_=scal.partition_broadcast(128).squeeze(1)), 16)
            s_misc.inc(gp.dma_start(
                out=bout_bc.ap(), in_=b_out.partition_broadcast(128).squeeze(1)), 16)

        @blk.scalar
        def _(sc):
            sc.wait_ge(s_misc.h, s_misc.n)
            ins = sc.activation(ebc.ap(), ebc.ap(), ACT.Exp)
            sself.inc(ins)

        @blk.vector
        def _(vec):
            if DEBUG[0]:
                vec.memset(dbgsb.ap(), 0.0)
                vec.memset(dbgp2.ap(), 0.0)
            vec.wait_ge(sself.h, sself.n)
            vec.tensor_scalar(negm.ap(), ebc.ap(), 80.0, -1.0,
                              ALU.min, ALU.mult)
            vec.tensor_tensor(lnm.ap()[:, 0:8], ebc.ap()[:, 0:8],
                              ebc.ap()[:, 8:16], ALU.max)
            vedge(vec)
            vec.tensor_tensor(lnm.ap()[:, 0:4], lnm.ap()[:, 0:4],
                              lnm.ap()[:, 4:8], ALU.max)
            vedge(vec)
            vec.tensor_tensor(lnm.ap()[:, 0:2], lnm.ap()[:, 0:2],
                              lnm.ap()[:, 2:4], ALU.max)
            vedge(vec)
            vec.tensor_tensor(lnm.ap()[:, 0:1], lnm.ap()[:, 0:1],
                              lnm.ap()[:, 1:2], ALU.max)
            vedge(vec)
            ins = vec.tensor_scalar(negmsh.ap(), 
                                    lnm.ap()[:, 0:1].broadcast_to([128, 8]),
                                    80.0, -1.0, ALU.min, ALU.mult)
            s_set.inc(ins)

        SETUP = s_set.n

        with ExitStack() as p12s:
            e12 = p12s.enter_context
            v_sb = e12(nc.sbuf_tensor("v_sb", [128, NKT, H, 65], BF16))

            with ExitStack() as p1s:
                e1 = p1s.enter_context
                xb = e1(nc.sbuf_tensor("xb", [128, 2, D], BF16))
                xt_sb = e1(nc.sbuf_tensor("xt_sb", [128, 2, 8, 128], BF16))
                wkv_sb = e1(nc.sbuf_tensor("wkv_sb", [128, 8, 2 * D], BF16))
                wqp_sb = e1(nc.sbuf_tensor("wqp_sb", [128, 8, D], BF16))
                ksq = e1(nc.sbuf_tensor("ksq", [128, D], F32))
                qsq = e1(nc.sbuf_tensor("qsq", [128, D], F32))
                kst = e1(nc.sbuf_tensor("kst", [128, 4, H], F32))
                qst = e1(nc.sbuf_tensor("qst", [128, 4, H], F32))
                khat = e1(nc.sbuf_tensor("khat", [128, 2, D], BF16))
                qhat = e1(nc.sbuf_tensor("qhat", [128, 2, D], BF16))
                ktr = e1(nc.sbuf_tensor("ktr", [128, 2, 8, 128], BF16))
                qtr = e1(nc.sbuf_tensor("qtr", [128, 2, 8, 128], BF16))
                ptx = e1(nc.psum_tensor("ptx", [128, 1024], F32))
                kv_ps = e1(nc.psum_tensor("kv_ps", [128, 2048], F32))
                q_ps = e1(nc.psum_tensor("q_ps", [128, 1024], F32))

                @blk.gpsimd
                def _(gp):
                    for cch in range(8):
                        s_w.inc(gp.dma_start(
                            out=wkv_sb.ap()[:, cch, :],
                            in_=w_kv[cch * 128:(cch + 1) * 128, :]), 16)
                        s_w.inc(gp.dma_start(
                            out=wqp_sb.ap()[:, cch, :],
                            in_=w_q[cch * 128:(cch + 1) * 128, :]), 16)

                @blk.vector
                def _(vec):
                    ins = vec.memset(v_sb.ap()[:, :, :, 64], 1.0)
                    s_set.inc(ins)
                P1SET = s_set.n

                grp = [0]

                def pe_tgroup(t, srcs):
                    g = grp[0]
                    grp[0] = g + 1
                    half = g % 2
                    if g >= 2:
                        t.wait_ge(s_xte.h, g - 1)
                    ins = None
                    for j, ap_ in enumerate(srcs):
                        ins = t.matmul(
                            ptx.ap()[:, half * 512 + j * 128:
                                     half * 512 + j * 128 + 128],
                            ap_, eye_sb.ap(), start=True, stop=True)
                    s_xt.inc(ins)
                    return g

                def vec_egroup(vec, g, dst_ap):
                    vec.wait_ge(s_xt.h, g + 1)
                    half = g % 2
                    src = ptx.ap()[:, half * 512:(half + 1) * 512].rearrange(
                        "p (c f) -> p c f", f=128)
                    ins = vec.tensor_copy(dst_ap, src)
                    s_xte.inc(ins)

                def gbase(kt):
                    return 6 * min(kt, NQT) + 4 * max(0, kt - NQT)

                for kt in range(NKT):
                    par = kt % 2
                    isq = kt < NQT
                    CQ = min(kt, NQT)
                    sq_base = kt + CQ
                    nrm_base = 2 * kt + CQ
                    gb = gbase(kt)
                    assert grp[0] == gb, (kt, grp[0], gb)

                    @blk.gpsimd
                    def _(gp):
                        if kt >= 2:
                            gp.wait_ge(s_xt.h, gbase(kt - 2) + 2)
                        ins = gp.dma_start(out=xb.ap()[:, par, :],
                                           in_=x[kt * 128:(kt + 1) * 128, :])
                        s_xp[kt % 2].inc(ins, 16)

                    @blk.tensor
                    def _(t):
                        if kt == 0:
                            t.wait_ge(s_misc.h, s_misc.n)
                            t.wait_ge(s_w.h, s_w.n)
                            t.wait_ge(s_set.h, P1SET)
                        t.wait_ge(s_xp[kt % 2].h, 16 * (kt // 2 + 1))
                        for g in range(2):
                            srcs = [xb.ap()[:, par,
                                            (g * 4 + j) * 128:(g * 4 + j + 1) * 128]
                                    for j in range(4)]
                            pe_tgroup(t, srcs)
                        t.wait_ge(s_xte.h, gb + 2)
                        if kt >= 1:
                            pb = 2 * (kt - 1) + min(kt - 1, NQT)
                            t.wait_ge(s_nrm.h, pb + 2)
                            t.wait_ge(s_sq.h, (kt - 1) + min(kt - 1, NQT) + 1)
                        ins = None
                        for cch in range(8):
                            for sl in range(4):
                                ins = t.matmul(
                                    kv_ps.ap()[:, sl * 512:(sl + 1) * 512],
                                    xt_sb.ap()[:, par, cch, :],
                                    wkv_sb.ap()[:, cch, sl * 512:(sl + 1) * 512],
                                    start=(cch == 0), stop=(cch == 7))
                        s_kv.inc(ins)
                        if isq:
                            if kt >= 1:
                                pb = 2 * (kt - 1) + min(kt - 1, NQT)
                                t.wait_ge(s_nrm.h, pb + 3)
                                t.wait_ge(s_sq.h,
                                          (kt - 1) + min(kt - 1, NQT) + 2)
                            ins = None
                            for cch in range(8):
                                for sl in range(2):
                                    ins = t.matmul(
                                        q_ps.ap()[:, sl * 512:(sl + 1) * 512],
                                        xt_sb.ap()[:, par, cch, :],
                                        wqp_sb.ap()[:, cch,
                                                    sl * 512:(sl + 1) * 512],
                                        start=(cch == 0), stop=(cch == 7))
                            s_qm.inc(ins)

                    @blk.scalar
                    def _(sc):
                        sc.wait_ge(s_kv.h, kt + 1)
                        s_sq.inc(sc.activation(ksq.ap(), kv_ps.ap()[:, 0:D],
                                               ACT.Square))
                        if isq:
                            sc.wait_ge(s_qm.h, kt + 1)
                            s_sq.inc(sc.activation(qsq.ap(), q_ps.ap(),
                                                   ACT.Square))
                        sc.wait_ge(s_vst.h, sq_base + 1)
                        sc.activation(kst.ap()[:, 1, :],
                                      kst.ap()[:, 0, :], ACT.Ln)
                        s_ssq.inc(sc.activation(kst.ap()[:, 1, :],
                                                kst.ap()[:, 1, :], ACT.Exp,
                                                scale=0.5))
                        if isq:
                            sc.wait_ge(s_vst.h, sq_base + 2)
                            sc.activation(qst.ap()[:, 1, :],
                                          qst.ap()[:, 0, :], ACT.Ln)
                            s_ssq.inc(sc.activation(qst.ap()[:, 1, :],
                                                    qst.ap()[:, 1, :], ACT.Exp,
                                                    scale=0.5))

                    @blk.vector
                    def _(vec):
                        for g in range(2):
                            vec_egroup(vec, gb + g,
                                       xt_sb.ap()[:, par, g * 4:(g + 1) * 4, :])
                        vec.wait_ge(s_kv.h, kt + 1)
                        if DEBUG[0] >= 2 and kt == 0:
                            vec.tensor_copy(dbgsb.ap()[:, 0, :],
                                            kv_ps.ap()[:, 0:64])
                            vec.tensor_copy(dbgsb.ap()[:, 1, :],
                                            kv_ps.ap()[:, D:D + 64])
                        ins = vec.tensor_copy(
                            v_sb.ap()[:, kt, :, 0:64],
                            kv_ps.ap()[:, D:2 * D].rearrange(
                                "p (h d) -> p h d", d=DH))
                        s_nrm.inc(ins)
                        vec.wait_ge(s_sq.h, sq_base + 1)
                        if DEBUG[0] >= 2 and kt == 0:
                            vec.tensor_copy(dbgsb.ap()[:, 2, :],
                                            ksq.ap()[:, 0:64])
                        vec.tensor_reduce(
                            kst.ap()[:, 0, :],
                            ksq.ap().rearrange("p (h d) -> p h d", d=DH),
                            AX, ALU.add)
                        s_vst.inc(vec.engine_nop())
                        vec.wait_ge(s_ssq.h, sq_base + 1)
                        if DEBUG[0] >= 2 and kt == 0:
                            vec.tensor_copy(dbgsb.ap()[:, 3, 0:16],
                                            kst.ap()[:, 0, :])
                            vec.tensor_copy(dbgsb.ap()[:, 3, 16:32],
                                            kst.ap()[:, 1, :])
                        vec.tensor_scalar(kst.ap()[:, 2, :], kst.ap()[:, 1, :],
                                          1e-12, None, ALU.max)
                        vrecip(vec, kst.ap()[:, 3, :], kst.ap()[:, 2, :])
                        if DEBUG[0] >= 2 and kt == 0:
                            vec.tensor_copy(dbgsb.ap()[:, 3, 32:48],
                                            kst.ap()[:, 3, :])
                        if kt >= 2:
                            vec.wait_ge(s_xt.h, gbase(kt - 2) + 4)
                        rb = kst.ap()[:, 3, :].unsqueeze(2).broadcast_to(
                            [128, H, DH])
                        ins = vec.tensor_tensor(
                            khat.ap()[:, par, :].rearrange(
                                "p (h d) -> p h d", d=DH),
                            kv_ps.ap()[:, 0:D].rearrange(
                                "p (h d) -> p h d", d=DH),
                            rb, ALU.mult)
                        s_nrm.inc(ins)
                        if isq:
                            vec.wait_ge(s_sq.h, sq_base + 2)
                            vec.tensor_reduce(
                                qst.ap()[:, 0, :],
                                qsq.ap().rearrange("p (h d) -> p h d", d=DH),
                                AX, ALU.add)
                            s_vst.inc(vec.engine_nop())
                            vec.wait_ge(s_ssq.h, sq_base + 2)
                            vec.tensor_scalar(qst.ap()[:, 2, :],
                                              qst.ap()[:, 1, :],
                                              1e-12, None, ALU.max)
                            vrecip(vec, qst.ap()[:, 3, :], qst.ap()[:, 2, :])
                            vec.tensor_tensor(qst.ap()[:, 3, :],
                                              qst.ap()[:, 3, :],
                                              ebc.ap(), ALU.mult)
                            if kt >= 2:
                                vec.wait_ge(s_xt.h, gbase(kt - 2) + 6)
                            rqb = qst.ap()[:, 3, :].unsqueeze(2).broadcast_to(
                                [128, H, DH])
                            ins = vec.tensor_tensor(
                                qhat.ap()[:, par, :].rearrange(
                                    "p (h d) -> p h d", d=DH),
                                q_ps.ap().rearrange("p (h d) -> p h d", d=DH),
                                rqb, ALU.mult)
                            s_nrm.inc(ins)

                    @blk.tensor
                    def _(t):
                        t.wait_ge(s_nrm.h, nrm_base + 2)
                        for g in range(2):
                            srcs = [khat.ap()[:, par,
                                              (g * 4 + j) * 128:
                                              (g * 4 + j + 1) * 128]
                                    for j in range(4)]
                            pe_tgroup(t, srcs)
                        if isq:
                            t.wait_ge(s_nrm.h, nrm_base + 3)
                            for g in range(2):
                                srcs = [qhat.ap()[:, par,
                                                  (g * 4 + j) * 128:
                                                  (g * 4 + j + 1) * 128]
                                        for j in range(4)]
                                pe_tgroup(t, srcs)

                    @blk.vector
                    def _(vec):
                        if kt >= 2:
                            vec.wait_ge(s_ktwp[kt % 2].h, 32 * (kt // 2))
                        if DEBUG[0] >= 2 and kt == 0:
                            vec.tensor_copy(dbgsb.ap()[:, 4, :],
                                            khat.ap()[:, par, 0:64])
                        for g in range(2):
                            vec_egroup(vec, gb + 2 + g,
                                       ktr.ap()[:, par, g * 4:(g + 1) * 4, :])
                        if DEBUG[0] >= 2 and kt == 0:
                            vec.tensor_copy(dbgsb.ap()[:, 5, :],
                                            ktr.ap()[:, par, 0, 0:64])
                        if isq:
                            if kt >= 2:
                                vec.wait_ge(s_qtwp[kt % 2].h, 32 * (kt // 2))
                            for g in range(2):
                                vec_egroup(vec, gb + 4 + g,
                                           qtr.ap()[:, par,
                                                    g * 4:(g + 1) * 4, :])

                    @blk.sync
                    def _(sy):
                        sy.wait_ge(s_xte.h, gb + 4)
                        if DEBUG[0] >= 2 and kt < 2:
                            sy.dma_start(out=dbg["ktr01"][kt, :, :, :],
                                         in_=ktr.ap()[:, par, :, :]).then_inc(
                                s_xr.h, 16)
                            s_xr.n += 16
                        for hseg in range(2):
                            src = ktr.ap()[:, par, hseg * 4:(hseg + 1) * 4, :]
                            dst = ktr_d.transpose([1, 0, 2])[
                                :, hseg * 4:(hseg + 1) * 4,
                                kt * 128:(kt + 1) * 128]
                            s_ktwp[kt % 2].inc(sy.dma_start(out=dst, in_=src), 16)
                        if isq:
                            sy.wait_ge(s_xte.h, gb + 6)
                            for hseg in range(2):
                                src = qtr.ap()[:, par,
                                               hseg * 4:(hseg + 1) * 4, :]
                                dst = qtr_d.transpose([1, 0, 2])[
                                    :, hseg * 4:(hseg + 1) * 4,
                                    kt * 128:(kt + 1) * 128]
                                s_qtwp[kt % 2].inc(sy.dma_start(out=dst, in_=src), 16)

                    pass

                @blk.vector
                def _(vec):
                    p1bar.inc(vec.engine_nop())

                @blk.scalar
                def _(sc):
                    p1bar.inc(sc.nop())

                @blk.sync
                def _(sy):
                    sy.wait_ge(s_ktwp[0].h, s_ktwp[0].n)
                    sy.wait_ge(s_ktwp[1].h, s_ktwp[1].n)
                    sy.wait_ge(s_qtwp[0].h, s_qtwp[0].n)
                    sy.wait_ge(s_qtwp[1].h, s_qtwp[1].n)
                    if DEBUG[0]:
                        sy.wait_ge(s_nrm.h, s_nrm.n)
                        for nm2, t2 in (("v", v_sb), ("kst", kst),
                                        ("khat", khat)):
                            sy.dma_start(out=dbg[nm2], in_=t2.ap()).then_inc(
                                s_xr.h, 16)
                            s_xr.n += 16
                        sy.wait_ge(s_xr.h, s_xr.n)
                    p1bar.inc(sy.nop())

                @blk.gpsimd
                def _(gp):
                    gp.wait_ge(s_xp[0].h, s_xp[0].n)
                    gp.wait_ge(s_xp[1].h, s_xp[1].n)
                    gp.wait_ge(s_w.h, s_w.n)
                    gp.wait_ge(s_misc.h, s_misc.n)
                    p1bar.inc(gp.engine_nop())

            QS2 = 512
            NQS2 = QTOK // QS2
            NT2 = QS2 // 128
            with ExitStack() as p2s:
                e2 = p2s.enter_context
                kpair = e2(nc.sbuf_tensor("kpair", [128, 2, SEQ], BF16))
                qpair = e2(nc.sbuf_tensor("qpair", [128, 2, QS2], BF16))
                ptile = e2(nc.sbuf_tensor("ptile", [128, 3, 2, QS2], BF16))
                ofm = e2(nc.sbuf_tensor("ofm", [65, 2, QS2], BF16))
                orc = e2(nc.sbuf_tensor("orc", [128, 1], F32))
                otok = e2(nc.sbuf_tensor("otok", [128, NT2, H, 64], BF16))
                wout_sb = e2(nc.sbuf_tensor("wout_sb", [128, 8, D], BF16))
                xresid = e2(nc.sbuf_tensor("xresid", [128, 2, D], F32))
                r1b = e2(nc.sbuf_tensor("r1b", [128, 2, D], F32))
                sAB = e2(nc.psum_tensor("sAB", [128, 2, 2, 512], F32))
                oA = e2(nc.psum_tensor("oA", [128, 512], F32))
                oB = e2(nc.psum_tensor("oB", [128, 512], F32))
                tailp = e2(nc.psum_tensor("tailp", [128, 1024], F32))
                oX = {0: oA, 1: oB}

                @blk.gpsimd
                def _(gp):
                    gp.wait_ge(p1bar.h, 4)
                    for cch in range(8):
                        s_w2.inc(gp.dma_start(
                            out=wout_sb.ap()[:, cch, :],
                            in_=w_out[cch * 128:(cch + 1) * 128, :]), 16)
                    p2bar.inc(gp.engine_nop())

                first = {"t": True, "sc": True, "sy": True, "v": True}

                def emit_kq_load(PIj):
                    qsj, pairj = divmod(PIj, NPAIR)
                    parj = PIj % 2

                    @blk.sync
                    def _(sy):
                        if first["sy"]:
                            sy.wait_ge(p1bar.h, 4)
                            first["sy"] = False
                        if PIj >= 2:
                            sy.wait_ge(s_simA.h, 32 * (PIj - 1))
                        ins = sy.dma_start(out=kpair.ap()[:, parj, :],
                                           in_=ktr_d[pairj, :, :])
                        s_kinp[parj].inc(ins, 16)
                        ins = sy.dma_start(
                            out=qpair.ap()[:, parj, :],
                            in_=qtr_d[pairj, :, qsj * QS2:(qsj + 1) * QS2])
                        s_qinp[parj].inc(ins, 16)

                emit_kq_load(0)
                for qs in range(NQS2):
                    for pair in range(NPAIR):
                        PI = qs * NPAIR + pair
                        par = PI % 2

                        @blk.tensor
                        def _(t):
                            if first["t"]:
                                t.wait_ge(p1bar.h, 4)
                                first["t"] = False
                            t.wait_ge(s_kinp[PI % 2].h, 16 * (PI // 2 + 1))
                            t.wait_ge(s_qinp[PI % 2].h, 16 * (PI // 2 + 1))
                            if pair == 0 and qs > 0:
                                t.wait_ge(s_r1.h, NT2 * qs)
                                t.wait_ge(s_obe.h, 2 * NT2 * qs)

                            def emit_sims(kt):
                                if kt >= 2:
                                    t.wait_ge(s_expA.h, PI * 32 + kt - 1)
                                elif PI > 0:
                                    t.wait_ge(s_expA.h, PI * 32)
                                ins = None
                                for hh in range(2):
                                    ins = t.matmul(
                                        sAB.ap()[:, kt % 2, hh, :],
                                        kpair.ap()[hh * 64:(hh + 1) * 64, par,
                                                   kt * 128:(kt + 1) * 128],
                                        qpair.ap()[hh * 64:(hh + 1) * 64,
                                                   par, :],
                                        start=True, stop=True,
                                        tile_position=(hh * 64, 0))
                                s_simA.inc(ins)

                            def emit_avs(kt):
                                t.wait_ge(s_expA.h, PI * 32 + kt + 1)
                                for hh in range(2):
                                    head = pair * 2 + hh
                                    if kt == 0 and PI >= 1:
                                        t.wait_ge(
                                            s_ofin.h,
                                            NT2 * (2 * (PI - 1) + hh + 1))
                                    ins = t.matmul(
                                        oX[hh].ap()[0:65, :],
                                        v_sb.ap()[:, kt, head, :],
                                        ptile.ap()[:, kt % 3, hh, :],
                                        start=(kt == 0), stop=(kt == NKT - 1))
                                    if hh == 1:
                                        s_avA.inc(ins)

                            for kt in range(NKT):
                                emit_sims(kt)
                                if kt >= 2:
                                    emit_avs(kt - 2)
                            emit_avs(NKT - 2)
                            emit_avs(NKT - 1)

                        @blk.scalar
                        def _(sc):
                            if first["sc"]:
                                sc.wait_ge(p1bar.h, 4)
                                first["sc"] = False
                            for kt in range(NKT):
                                sc.wait_ge(s_simA.h, PI * 32 + kt + 1)
                                if kt >= 3:
                                    sc.wait_ge(s_avA.h, PI * 32 + kt - 2)
                                ins = sc.activation(
                                    ptile.ap()[:, kt % 3, :, :].rearrange(
                                        "p a b -> p (a b)"),
                                    sAB.ap()[:, kt % 2, :, :].rearrange(
                                        "p a b -> p (a b)"),
                                    ACT.Exp,
                                    bias=negmsh.ap()[:, 0:1])
                                s_expA.inc(ins)

                        for hh in range(2):
                            head = pair * 2 + hh
                            obase = (2 * PI + hh) * NT2

                            @blk.scalar
                            def _(sc):
                                sc.wait_ge(s_avA.h, (PI + 1) * 32)
                                if PI >= 1:
                                    sc.wait_ge(s_otr.h,
                                               (2 * (PI - 1) + hh + 1) * NT2)
                                ins = sc.activation(ofm.ap()[0:65, hh, :],
                                                    oX[hh].ap()[0:65, :],
                                                    ACT.Copy)
                                s_oev.inc(ins)

                            @blk.tensor
                            def _(t):
                                t.wait_ge(s_oev.h, 2 * PI + hh + 1)
                                if PI == 0 and hh == 0 and qs == 0:
                                    pass
                                elif hh == 0 and pair == 0:
                                    t.wait_ge(s_obe.h, 2 * NT2 * qs)
                                for tq in range(NT2):
                                    gidx = obase + tq
                                    if gidx >= 2:
                                        t.wait_ge(s_ofin.h, gidx - 1)
                                    ins = t.matmul(
                                        tailp.ap()[:, (gidx % 2) * 512:
                                                   (gidx % 2) * 512 + 65],
                                        ofm.ap()[0:65, hh,
                                                 tq * 128:(tq + 1) * 128],
                                        eye_sb.ap()[0:65, 0:65],
                                        start=True, stop=True)
                                    s_otr.inc(ins)

                            @blk.vector
                            def _(vec):
                                for tq in range(NT2):
                                    vec.wait_ge(s_otr.h, obase + tq + 1)
                                    if qs > 0:
                                        vec.wait_ge(s_op.h, NT2 * qs)
                                    base = ((obase + tq) % 2) * 512
                                    vrecip(vec,
                                           orc.ap(),
                                           tailp.ap()[:, base + 64:base + 65])
                                    vedge(vec)
                                    ins = vec.tensor_scalar(
                                        otok.ap()[:, tq, head, :],
                                        tailp.ap()[:, base:base + 64],
                                        orc.ap(), None, ALU.mult)
                                    s_ofin.inc(ins)

                        if PI + 1 < NQS2 * NPAIR:
                            emit_kq_load(PI + 1)
                        emit_wq_unit()
                        emit_wq_unit()

                    for tq in range(NT2):
                        R = qs * NT2 + tq
                        par = R % 2

                        @blk.sync
                        def _(sy):
                            if R >= 2:
                                sy.wait_ge(s_r1.h, R - 1)
                            ins = sy.dma_start(out=xresid.ap()[:, par, :],
                                               in_=x[R * 128:(R + 1) * 128, :])
                            s_xrp[R % 2].inc(ins, 16)

                        og0 = (qs * NT2 + tq) * 2

                        @blk.tensor
                        def _(t):
                            if tq == 0:
                                t.wait_ge(s_ofin.h, (qs + 1) * 16 * NT2)
                                t.wait_ge(s_expA.h, (qs * NPAIR + NPAIR) * 32)
                            for g in range(2):
                                og = og0 + g
                                if og >= 2:
                                    t.wait_ge(s_obe.h, og - 1)
                                src_flat = otok.ap()[:, tq, :, :].rearrange(
                                    "p h d -> p (h d)")
                                ins = None
                                for j in range(4):
                                    cc = g * 4 + j
                                    ins = t.matmul(
                                        tailp.ap()[:, (og % 2) * 512 + j * 128:
                                                   (og % 2) * 512 + (j + 1) * 128],
                                        src_flat[:, cc * 128:(cc + 1) * 128],
                                        eye_sb.ap(), start=True, stop=True)
                                s_obt.inc(ins)

                        @blk.vector
                        def _(vec):
                            for g in range(2):
                                og = og0 + g
                                vec.wait_ge(s_obt.h, og + 1)
                                dst = otok.ap()[:, tq, :, :].rearrange(
                                    "p h d -> p (h d)").rearrange(
                                    "p (c f) -> p c f", f=128)[
                                    :, g * 4:(g + 1) * 4, :]
                                src = tailp.ap()[:, (og % 2) * 512:
                                                 (og % 2 + 1) * 512].rearrange(
                                    "p (c f) -> p c f", f=128)
                                ins = vec.tensor_copy(dst, src)
                                s_obe.inc(ins)

                        @blk.tensor
                        def _(t):
                            t.wait_ge(s_obe.h, og0 + 2)
                            if tq == 0:
                                t.wait_ge(s_oev.h, 2 * (qs * NPAIR + NPAIR))
                                t.wait_ge(s_w2.h, s_w2.n)
                            else:
                                t.wait_ge(s_r1.h, R)
                            src_flat = otok.ap()[:, tq, :, :].rearrange(
                                "p h d -> p (h d)")
                            ins = None
                            for cch in range(8):
                                for ds in range(2):
                                    ins = t.matmul(
                                        oX[ds].ap()[:, :],
                                        src_flat[:, cch * 128:(cch + 1) * 128],
                                        wout_sb.ap()[:, cch,
                                                     ds * 512:(ds + 1) * 512],
                                        start=(cch == 0), stop=(cch == 7))
                            s_op.inc(ins)

                        @blk.vector
                        def _(vec):
                            vec.wait_ge(s_op.h, R + 1)
                            vec.wait_ge(s_xrp[R % 2].h, 16 * (R // 2 + 1))
                            if R >= 2:
                                vec.wait_ge(s_r1wp[R % 2].h, 16 * (R // 2))
                            for ds in range(2):
                                vec.tensor_tensor(
                                    r1b.ap()[:, par, ds * 512:(ds + 1) * 512],
                                    oX[ds].ap(),
                                    bout_bc.ap()[:, ds * 512:(ds + 1) * 512],
                                    ALU.add)
                            ins = vec.tensor_tensor(r1b.ap()[:, par, :],
                                                    r1b.ap()[:, par, :],
                                                    xresid.ap()[:, par, :],
                                                    ALU.add)
                            s_r1.inc(ins)

                        @blk.sync
                        def _(sy):
                            sy.wait_ge(s_r1.h, R + 1)
                            ins = sy.dma_start(
                                out=r1_d[R * 128:(R + 1) * 128, :],
                                in_=r1b.ap()[:, par, :])
                            s_r1wp[R % 2].inc(ins, 16)

                        emit_wq_unit()

                @blk.vector
                def _(vec):
                    p2bar.inc(vec.engine_nop())

                @blk.scalar
                def _(sc):
                    p2bar.inc(sc.nop())

                @blk.sync
                def _(sy):
                    sy.wait_ge(s_r1wp[0].h, s_r1wp[0].n)
                    sy.wait_ge(s_r1wp[1].h, s_r1wp[1].n)
                    p2bar.inc(sy.nop())

                while wq_next[0] < N_WQ:
                    emit_wq_unit()

        with ExitStack() as p3s:
            e3 = p3s.enter_context
            wgT = e3(nc.sbuf_tensor("wgT", [128, 8, IFF], BF16))
            wuT = e3(nc.sbuf_tensor("wuT", [128, 8, IFF], BF16))
            wdT = e3(nc.sbuf_tensor("wdT", [128, 22, D], BF16))
            wqin = e3(nc.sbuf_tensor("wqin", [128, 1, 1024], BF16))
            wdin = e3(nc.sbuf_tensor("wdin", [128, 1, 4, 128], BF16))
            rln = e3(nc.sbuf_tensor("rln", [128, 2, D], F32))
            y1b = e3(nc.sbuf_tensor("y1b", [128, 2, D], F32))
            xqs = e3(nc.sbuf_tensor("xqs", [128, 1, D], BF16))
            xqt = e3(nc.sbuf_tensor("xqt", [128, 2, 8, 128], BF16))
            sgb = e3(nc.sbuf_tensor("sgb", [128, 1, 512], F32))
            hbuf = e3(nc.sbuf_tensor("hbuf", [128, 512], F32))
            hqb = e3(nc.sbuf_tensor("hqb", [128, 1, 512], BF16))
            hqt = e3(nc.sbuf_tensor("hqt", [128, 1, 4, 128], BF16))
            g_ps = e3(nc.psum_tensor("g_ps", [128, 1024], F32))
            u_ps = e3(nc.psum_tensor("u_ps", [128, 1024], F32))
            d_ps = e3(nc.psum_tensor("d_ps", [128, 1024], F32))
            trp = e3(nc.psum_tensor("trp", [128, 1024], F32))

            grp3 = [0]
            tre_of_grp = {}

            def pe_tgroup3(t, srcs):
                g = grp3[0]
                grp3[0] = g + 1
                half = g % 2
                if g >= 2:
                    t.wait_ge(s_tre.h, g - 1)
                ins = None
                for j, ap_ in enumerate(srcs):
                    ins = t.matmul(
                        trp.ap()[:, half * 512 + j * 128:
                                 half * 512 + j * 128 + 128],
                        ap_, eye_sb.ap(), start=True, stop=True)
                s_trp.inc(ins)
                return g

            def vec_egroup3(vec, g, dst_ap, nsub=4):
                vec.wait_ge(s_trp.h, g + 1)
                half = g % 2
                src = trp.ap()[:, half * 512:half * 512 + nsub * 128].rearrange(
                    "p (c f) -> p c f", f=128)
                ins = vec.tensor_copy(dst_ap, src)
                s_tre.inc(ins)

            def sc_egroup3(sc, g, dst_ap, nsub=4):
                sc.wait_ge(s_trp.h, g + 1)
                half = g % 2
                src = trp.ap()[:, half * 512:half * 512 + nsub * 128].rearrange(
                    "p (c f) -> p c f", f=128)
                ins = sc.activation(dst_ap, src, ACT.Copy)
                s_tre.inc(ins)

            firstp3 = {"t": True, "sy": True, "v": True, "sc": True}

            load_i = [0]
            wq_ready = {}

            def p3_load_unit(name, ti, co, w, dst_slot):
                li = load_i[0]
                load_i[0] = li + 1
                u = WQ_UNIT_IDX[(name, ti, co)]

                @blk.sync
                def _(sy):
                    if firstp3["sy"]:
                        sy.wait_ge(p2bar.h, 4)
                        sy.wait_ge(s_wfwp[0].h, s_wfwp[0].n)
                        sy.wait_ge(s_wfwp[1].h, s_wfwp[1].n)
                        firstp3["sy"] = False
                    if li >= 1:
                        sy.wait_ge(s_trp.h, wq_ready.get(("ld", li - 1), 0))
                    ins = sy.dma_start(
                        out=dst_slot,
                        in_=wq_out_d[name][ti * 128:(ti + 1) * 128, co:co + w])
                    s_wqi.inc(ins, 16)
                return li

            wgwu = {"wg": (wgT, wgq_d), "wu": (wuT, wuq_d)}
            order = []
            for sl in range(6):
                lo = sl * 4
                hi = min(lo + 4, 22)
                for nm in ("wg", "wu"):
                    for ti in range(lo, hi):
                        order.append((nm, ti))
            for nm, ti in order:
                par = 0
                li = p3_load_unit(nm, ti, 0, 1024, wqin.ap()[:, 0, :])
                gidx = []

                @blk.tensor
                def _(t):
                    if firstp3["t"]:
                        t.wait_ge(p2bar.h, 4)
                        firstp3["t"] = False
                    t.wait_ge(s_wqi.h, 16 * (li + 1))
                    for g in range(2):
                        srcs = [wqin.ap()[:, par,
                                          (g * 4 + j) * 128:(g * 4 + j + 1) * 128]
                                for j in range(4)]
                        gidx.append(pe_tgroup3(t, srcs))

                @blk.vector
                def _(vec):
                    if firstp3["v"]:
                        vec.wait_ge(p2bar.h, 4)
                        firstp3["v"] = False
                    dstT = wgT if nm == "wg" else wuT
                    for g in range(2):
                        vec_egroup3(vec, gidx[g],
                                    dstT.ap()[:, g * 4:(g + 1) * 4,
                                              ti * 128:(ti + 1) * 128])
                wq_ready[("ld", li)] = s_trp.n
                wq_ready[(nm, ti)] = s_tre.n

            for ti in range(8):
                for gc in range(6):
                    nch = 4 if gc < 5 else 2
                    par = 0
                    li = load_i[0]
                    load_i[0] += 1
                    u = WQ_UNIT_IDX[("wd", ti, (gc * 4 * 128 // 1024) * 1024)] \
                        if False else None

                    @blk.sync
                    def _(sy):
                        if li >= 1:
                            sy.wait_ge(s_trp.h, wq_ready.get(("ld", li - 1), 0))
                        for j in range(nch):
                            cc = gc * 4 + j
                            ins = sy.dma_start(
                                out=wdin.ap()[:, par, j, :],
                                in_=wdq_d[ti * 128:(ti + 1) * 128,
                                          cc * 128:(cc + 1) * 128])
                            s_wqi.inc(ins, 16)

                    gidx = []

                    @blk.tensor
                    def _(t):
                        t.wait_ge(s_wqi.h, s_wqi.n)
                        srcs = [wdin.ap()[:, par, j, :] for j in range(nch)]
                        gidx.append(pe_tgroup3(t, srcs))

                    @blk.vector
                    def _(vec):
                        vec_egroup3(vec, gidx[0],
                                    wdT.ap()[:, gc * 4:gc * 4 + nch,
                                             ti * 128:(ti + 1) * 128],
                                    nsub=nch)
                    wq_ready[("ld", li)] = s_trp.n
            WD_DONE = s_tre.n

            xqt_done = {}
            hq_trp = {}
            xqt_trp = {}

            def emit_rd(j):
                @blk.sync
                def _(sy):
                    if j == 0:
                        sy.wait_ge(s_r1wp[0].h, s_r1wp[0].n)
                        sy.wait_ge(s_r1wp[1].h, s_r1wp[1].n)
                    if j >= 2:
                        sy.wait_ge(s_xq3.h, j - 1)
                        sy.wait_ge(s_outp[j % 2].h, 16 * (j // 2))
                    ins = sy.dma_start(out=rln.ap()[:, j % 2, :],
                                       in_=r1_d[j * 128:(j + 1) * 128, :])
                    s_r1ip[j % 2].inc(ins, 16)

            def emit_ln1_stats(j):
                @blk.vector
                def _(vec):
                    vec.wait_ge(s_r1ip[j % 2].h, 16 * (j // 2 + 1))
                    ln_stats(vec, rln.ap()[:, j % 2, :])

            def emit_ln1_sqrt(j):
                @blk.scalar
                def _(sc):
                    if firstp3["sc"]:
                        sc.wait_ge(p2bar.h, 4)
                        firstp3["sc"] = False
                    ln_sqrt(sc, vself.n)

            def emit_ln1_finqx(j):
                @blk.vector
                def _(vec):
                    ln_fin(vec, rln.ap()[:, j % 2, :], y1b.ap()[:, j % 2, :],
                           sself.n)
                    if j >= 1:
                        vec.wait_ge(s_trp.h, xqt_trp.get(j - 1, 0))
                    ins = quant_slab(vec, y1b.ap()[:, j % 2, :],
                                     xqs.ap()[:, 0, :], D)
                    s_xq3.inc(ins)

            def emit_xqT(j):
                gidx = []

                @blk.tensor
                def _(t):
                    t.wait_ge(s_xq3.h, j + 1)
                    for g in range(2):
                        srcs = [xqs.ap()[:, 0,
                                         (g * 4 + k) * 128:
                                         (g * 4 + k + 1) * 128]
                                for k in range(4)]
                        gidx.append(pe_tgroup3(t, srcs))
                xqt_trp[j] = s_trp.n
                return gidx

            def emit_xqT_evac(j, gidx):
                @blk.scalar
                def _(sc):
                    for g in range(2):
                        sc_egroup3(sc, gidx[g],
                                   xqt.ap()[:, j % 2, g * 4:(g + 1) * 4, :])
                xqt_done[j] = s_tre.n

            def emit_gu(t_i, sl):
                W = 512 if sl < 5 else 256
                SL = 6 * t_i + sl
                p2_ = sl % 2
                par = t_i % 2

                @blk.tensor
                def _(t):
                    if sl == 0:
                        t.wait_ge(s_tre.h, xqt_done[t_i])
                    if t_i == 0:
                        t.wait_ge(s_tre.h,
                                  max(wq_ready[("wg", min(sl * 4 + 3, 21))],
                                      wq_ready[("wu", min(sl * 4 + 3, 21))]))
                    if SL >= 2:
                        t.wait_ge(s_sil.h, SL - 1)
                    ins = None
                    for cch in range(8):
                        ins = t.matmul(
                            g_ps.ap()[:, p2_ * 512:p2_ * 512 + W],
                            xqt.ap()[:, par, cch, :],
                            wgT.ap()[:, cch, sl * 512:sl * 512 + W],
                            start=(cch == 0), stop=(cch == 7))
                    s_gu.inc(ins)
                    if SL >= 2:
                        t.wait_ge(s_h.h, SL - 1)
                    for cch in range(8):
                        ins = t.matmul(
                            u_ps.ap()[:, p2_ * 512:p2_ * 512 + W],
                            xqt.ap()[:, par, cch, :],
                            wuT.ap()[:, cch, sl * 512:sl * 512 + W],
                            start=(cch == 0), stop=(cch == 7))
                    s_gu.inc(ins)

            def emit_silu(t_i, sl):
                W = 512 if sl < 5 else 256
                SL = 6 * t_i + sl
                p2_ = sl % 2

                @blk.scalar
                def _(sc):
                    sc.wait_ge(s_gu.h, 2 * SL + 1)
                    if SL >= 1:
                        sc.wait_ge(s_h.h, SL)
                    ins = sc.activation(sgb.ap()[:, 0, 0:W],
                                        g_ps.ap()[:, p2_ * 512:p2_ * 512 + W],
                                        ACT.Silu)
                    s_sil.inc(ins)

            def emit_hq(t_i, sl):
                W = 512 if sl < 5 else 256
                SL = 6 * t_i + sl
                p2_ = sl % 2

                @blk.vector
                def _(vec):
                    vec.wait_ge(s_gu.h, 2 * SL + 2)
                    vec.wait_ge(s_sil.h, SL + 1)
                    ins = vec.tensor_tensor(
                        hbuf.ap()[:, 0:W], sgb.ap()[:, 0, 0:W],
                        u_ps.ap()[:, p2_ * 512:p2_ * 512 + W], ALU.mult)
                    s_h.inc(ins)
                    if SL >= 1:
                        vec.wait_ge(s_trp.h, hq_trp.get(SL - 1, 0))
                    ins = quant_slab(vec, hbuf.ap()[:, 0:W],
                                     hqb.ap()[:, 0, 0:W], W)
                    s_hq.inc(ins)

            def emit_hqT(t_i, sl):
                W = 512 if sl < 5 else 256
                SL = 6 * t_i + sl
                nch = W // 128
                gidx2 = []

                @blk.tensor
                def _(t):
                    t.wait_ge(s_hq.h, SL + 1)
                    srcs = [hqb.ap()[:, 0, k * 128:(k + 1) * 128]
                            for k in range(nch)]
                    gidx2.append(pe_tgroup3(t, srcs))
                hq_trp[SL] = s_trp.n
                return gidx2

            def emit_hq_evac(t_i, sl, gidx2):
                W = 512 if sl < 5 else 256
                nch = W // 128

                @blk.scalar
                def _(sc):
                    sc_egroup3(sc, gidx2[0], hqt.ap()[:, 0, 0:nch, :],
                               nsub=nch)
                return s_tre.n

            def emit_down(t_i, sl, hq_evac):
                W = 512 if sl < 5 else 256
                nch = W // 128

                @blk.tensor
                def _(t):
                    t.wait_ge(s_tre.h, hq_evac)
                    if t_i == 0 and sl == 0:
                        t.wait_ge(s_tre.h, WD_DONE)
                    if sl == 0 and t_i >= 1:
                        t.wait_ge(s_y2e.h, t_i)
                    ins = None
                    for c4 in range(nch):
                        ic = sl * 4 + c4
                        for ds in range(2):
                            ins = t.matmul(
                                d_ps.ap()[:, ds * 512:(ds + 1) * 512],
                                hqt.ap()[:, 0, c4, :],
                                wdT.ap()[:, ic, ds * 512:(ds + 1) * 512],
                                start=(ic == 0), stop=(ic == 21))
                    if sl == 5:
                        s_dn.inc(ins)

            def emit_tail(t_i):
                par = t_i % 2

                @blk.vector
                def _(vec):
                    vec.wait_ge(s_dn.h, t_i + 1)
                    ins = vec.tensor_tensor(rln.ap()[:, par, :], d_ps.ap(),
                                            y1b.ap()[:, par, :], ALU.add)
                    s_y2e.inc(ins)
                    ln_stats(vec, rln.ap()[:, par, :])

                @blk.scalar
                def _(sc):
                    ln_sqrt(sc, vself.n)

                @blk.vector
                def _(vec):
                    ins = ln_fin(vec, rln.ap()[:, par, :],
                                 rln.ap()[:, par, :], sself.n)
                    s_ln2.inc(ins)

                @blk.sync
                def _(sy):
                    sy.wait_ge(s_ln2.h, t_i + 1)
                    ins = sy.dma_start(out=y[t_i * 128:(t_i + 1) * 128, :],
                                       in_=rln.ap()[:, par, :])
                    s_outp[t_i % 2].inc(ins, 16)

            emit_rd(0)
            emit_ln1_stats(0)
            emit_ln1_sqrt(0)
            emit_ln1_finqx(0)
            emit_xqT_evac(0, emit_xqT(0))

            for t_i in range(NQT):
                nxt = t_i + 1 if t_i + 1 < NQT else None
                if nxt is not None:
                    emit_rd(nxt)
                emit_gu(t_i, 0)
                emit_silu(t_i, 0)
                emit_hq(t_i, 0)
                emit_gu(t_i, 1)
                emit_silu(t_i, 1)
                ev0 = emit_hq_evac(t_i, 0, emit_hqT(t_i, 0))
                emit_down(t_i, 0, ev0)
                if nxt is not None:
                    emit_ln1_stats(nxt)
                    emit_ln1_sqrt(nxt)
                emit_hq(t_i, 1)
                emit_gu(t_i, 2)
                emit_silu(t_i, 2)
                ev1 = emit_hq_evac(t_i, 1, emit_hqT(t_i, 1))
                emit_down(t_i, 1, ev1)
                if nxt is not None:
                    emit_ln1_finqx(nxt)
                emit_hq(t_i, 2)
                emit_gu(t_i, 3)
                emit_silu(t_i, 3)
                ev2 = emit_hq_evac(t_i, 2, emit_hqT(t_i, 2))
                emit_down(t_i, 2, ev2)
                if nxt is not None:
                    emit_xqT_evac(nxt, emit_xqT(nxt))
                emit_hq(t_i, 3)
                emit_gu(t_i, 4)
                emit_silu(t_i, 4)
                ev3 = emit_hq_evac(t_i, 3, emit_hqT(t_i, 3))
                emit_down(t_i, 3, ev3)
                emit_hq(t_i, 4)
                emit_gu(t_i, 5)
                emit_silu(t_i, 5)
                ev4 = emit_hq_evac(t_i, 4, emit_hqT(t_i, 4))
                emit_down(t_i, 4, ev4)
                emit_hq(t_i, 5)
                ev5 = emit_hq_evac(t_i, 5, emit_hqT(t_i, 5))
                emit_down(t_i, 5, ev5)
                emit_tail(t_i)

            @blk.sync
            def _(sy):
                sy.wait_ge(s_outp[0].h, s_outp[0].n)
                sy.wait_ge(s_outp[1].h, s_outp[1].n)
                if DEBUG[0]:
                    for nm3, t3 in (("wgT", wgT), ("wdT", wdT), ("xqt", xqt),
                                    ("y1b", y1b), ("hqt", hqt), ("rlnv", rln)):
                        sy.dma_start(out=dbg[nm3], in_=t3.ap()).then_inc(
                            s_xr.h, 16)
                        s_xr.n += 16
                    sy.dma_start(out=dbg["st0"], in_=dbgsb.ap()).then_inc(
                        s_xr.h, 16)
                    s_xr.n += 16
                    sy.dma_start(out=dbg["p2"], in_=dbgp2.ap()).then_inc(
                        s_xr.h, 16)
                    s_xr.n += 16
                    for name, srcd in (("ktr", ktr_d), ("qtr", qtr_d),
                                       ("r1", r1_d), ("wgq", wgq_d)):
                        sy.dma_start(out=dbg[name], in_=srcd).then_inc(
                            s_xr.h, 16)
                        s_xr.n += 16
                    sy.wait_ge(s_xr.h, s_xr.n)

    return nc


_NC_CACHE = [None]


def kernel(L0_attn_x, L0_attn_w_q, L0_attn_w_kv, L0_attn_w_out, L0_attn_b_out,
           L0_attn_scale, L0_ffn_w_gate, L0_ffn_w_up, L0_ffn_w_down):
    xin = np.ascontiguousarray(np.asarray(L0_attn_x, dtype=np.float32))
    w_q = np.ascontiguousarray(np.asarray(L0_attn_w_q, dtype=np.float32))
    w_kv = np.ascontiguousarray(np.asarray(L0_attn_w_kv, dtype=np.float32))
    w_out = np.ascontiguousarray(np.asarray(L0_attn_w_out, dtype=np.float32))
    b_out = np.asarray(L0_attn_b_out, dtype=np.float32).reshape(1, D)
    scal = np.asarray(L0_attn_scale, dtype=np.float32).reshape(1, H)
    wg = np.ascontiguousarray(np.asarray(L0_ffn_w_gate, dtype=np.float32))
    wu = np.ascontiguousarray(np.asarray(L0_ffn_w_up, dtype=np.float32))
    wd = np.ascontiguousarray(np.asarray(L0_ffn_w_down, dtype=np.float32))
    eyev = np.eye(128, dtype=np.float32)

    if _NC_CACHE[0] is None:
        _NC_CACHE[0] = build_core()
    nc = _NC_CACHE[0]

    in_maps = []
    for c in range(8):
        b, qh = c // 2, c % 2
        xc = np.ascontiguousarray(np.roll(xin[b], -qh * QTOK, axis=0))
        in_maps.append({
            "x": xc, "w_q": w_q, "w_kv": w_kv, "w_out": w_out,
            "b_out": b_out, "scal": scal, "wg": wg, "wu": wu, "wd": wd,
            "eye": eyev,
        })

    res = run_bass_kernel_spmd(nc, in_maps, core_ids=list(range(8)))
    out = np.empty((4, SEQ, D), dtype=np.float32)
    for c in range(8):
        b, qh = c // 2, c % 2
        out[b, qh * QTOK:(qh + 1) * QTOK] = res.results[c]["y"]
    return out
